# version 35
# baseline (speedup 1.0000x reference)
import contextlib
import numpy as np

import concourse.bass as bass
import concourse.mybir as mybir
from concourse.bass_utils import run_bass_kernel_spmd

F32 = mybir.dt.float32
BF16 = mybir.dt.bfloat16
AF = mybir.ActivationFunctionType
ALU = mybir.AluOpType
AX = mybir.AxisListType

S = 2048
D = 1024
NH = 16
NE = 4
HID = 4096
NTS = 16
NTT = 4
NDC = 8
NG = 4
NHC = 8
VW = NH * 65

ENGINES = ("sync", "gpsimd", "tensor", "vector", "scalar")


class _Plan:
    def __init__(self):
        self.blocks = [{e: [] for e in ENGINES}]

    def brk(self):
        self.blocks.append({e: [] for e in ENGINES})

    def emit(self, eng, fn):
        self.blocks[-1][eng].append(fn)

    def run(self, nc):
        for blk_ops in self.blocks:
            if not any(blk_ops[e] for e in ENGINES):
                continue
            with nc.Block() as blk:
                for eng in ENGINES:
                    ops = blk_ops[eng]
                    if ops:
                        getattr(blk, eng)(lambda e, ops=ops: [f(e) for f in ops])


def build_nc():
    nc = bass.Bass("TRN2", target_bir_lowering=False)

    x_h = nc.dram_tensor("x", [S, D], F32, kind="ExternalInput")
    wqkv_h = nc.dram_tensor("w_qkv", [3 * D, D], F32, kind="ExternalInput")
    wout_h = nc.dram_tensor("w_out", [D, D], F32, kind="ExternalInput")
    w1_h = nc.dram_tensor("w1", [NE, D, HID], F32, kind="ExternalInput")
    b1_h = nc.dram_tensor("b1", [NE, HID], F32, kind="ExternalInput")
    w2_h = nc.dram_tensor("w2", [NE, HID, D], F32, kind="ExternalInput")
    b2_h = nc.dram_tensor("b2", [NE, D], F32, kind="ExternalInput")
    wr_h = nc.dram_tensor("wr", [D, NE], F32, kind="ExternalInput")
    br_h = nc.dram_tensor("br", [NE], F32, kind="ExternalInput")
    y_h = nc.dram_tensor("y", [S, D], F32, kind="ExternalOutput")
    attn_sc = nc.dram_tensor("attn_sc", [S, D], F32, kind="Internal")
    wqkvbf_h = nc.dram_tensor("wqkv_bf", [3 * D, D], BF16, kind="Internal")
    woutbf_h = nc.dram_tensor("wout_bf", [D, D], BF16, kind="Internal")

    base = (nc.sbuf_base + 31) // 32 * 32

    def at(name, shape, dt, off):
        return nc.alloc_sbuf_tensor_at(name, shape, dt, offset=base + off)

    ident = at("ident", [128, 128], F32, 0)
    ones_f = at("ones_f", [1, 128], F32, 512)
    ones_p = at("ones_p", [128, 1], F32, 1024)
    oh = at("oh", [1, 16], BF16, 1056)
    cneg8 = at("cneg8", [128, 1], F32, 1120)
    c0125 = at("c0125", [128, 1], F32, 1152)
    ceps = at("ceps", [128, 1], F32, 1184)
    wr_sb = at("wr_sb", [128, 32], F32, 1216)
    br_sb = at("br_sb", [1, 4], F32, 1344)
    b1_sb = at("b1_sb", [128, 128], F32, 1376)
    hfake = at("hfake", [128, 128], BF16, 1888)
    s1a = at("s1a", [128, 16], F32, 2400)
    s2a = at("s2a", [128, 16], F32, 2464)
    m_a = at("m_a", [128, 16], F32, 2528)
    E2a = at("E2a", [128, 16], F32, 2592)
    mma = at("mma", [128, 16], F32, 2656)
    var_a = at("var_a", [128, 16], F32, 2720)
    std_a = at("std_a", [128, 16], F32, 2784)
    r_a = at("r_a", [128, 16], F32, 2848)
    nmr_a = at("nmr_a", [128, 16], F32, 2912)
    lgt = at("lgt", [128, 64], F32, 2976)
    br_em = at("br_em", [4, 1], F32, 1088)
    lgt_em = at("lgt_em", [4, 512], F32, 18208)
    xfm4 = at("xfm4", [128, 4096], F32, 57344)
    m1a = at("m1a", [128, 16], F32, 3232)
    nm1a = at("nm1a", [128, 16], F32, 3296)
    cmp = at("cmp", [128, 288], F32, 3360)
    cnt = at("cnt", [128, 64], F32, 4512)
    mask2 = at("mask2", [128, 64], F32, 4768)
    exa = at("exa", [128, 64], F32, 5024)
    g2a = at("g2a", [128, 64], F32, 5280)
    Sa = at("Sa", [128, 16], F32, 5536)
    rSa = at("rSa", [128, 16], F32, 5600)
    agate = at("agate", [128, 64], F32, 5664)
    aem = at("aem", [4, 2048], F32, 5920)
    masks = at("masks", [128, 2048], BF16, 14112)
    b1st = at("b1st", [32, 512], F32, 18208)
    cneg1 = at("cneg1", [128, 1], F32, 20256)
    ones_bf = at("ones_bf", [1, 64], BF16, 20288)
    cm05 = at("cm05", [128, 1], F32, 20416)

    xn_fm = at("xn_fm", [128, NDC * S], BF16, 20480)
    w_outT = at("w_outT", [128, 8192], BF16, 186880)
    b2_sb = at("b2_sb", [4, 1024], F32, 20480)
    g4_sb = at("g4_sb", [4, 1024], F32, 24576)
    tmp4 = at("tmp4", [4, 1024], F32, 28672)
    csum_sb = at("csum_sb", [1, 1024], F32, 32768)
    b2p_sb = at("b2p_sb", [4, 1024], F32, 36864)

    q_fm = at("q_fm", [128, 16384], BF16, 53248)
    k_fm = at("k_fm", [128, 16384], BF16, 86016)
    v_ext = at("v_ext", [128, NTS * VW], BF16, 118784)
    xsq0 = at("xsq0", [128, 1024], F32, 53248)
    xsq1 = at("xsq1", [128, 1024], F32, 57344)
    xsq2 = at("xsq2", [128, 1024], F32, 61440)
    xsq3 = at("xsq3", [128, 1024], F32, 65536)
    xfm = at("xfm", [128, 1024], F32, 69632)
    xnf0 = at("xnf0", [128, 1024], F32, 73728)
    xnf1 = at("xnf1", [128, 1024], F32, 77824)
    h_sb0 = at("h_sb0", [128, 4096], BF16, 53248)
    h_sb1 = at("h_sb1", [128, 4096], BF16, 61440)
    g_sb = at("g_sb", [1, 4096], BF16, 69632)
    w1g0 = at("w1g0", [128, 8192], BF16, 77824)
    w1g1 = at("w1g1", [128, 8192], BF16, 94208)
    w2g0 = at("w2g0", [128, 8192], BF16, 110592)
    w2g1 = at("w2g1", [128, 8192], BF16, 126976)
    xrg = [at(f"xrg{i}", [128, 4096], F32, 53248 + 16384 * i) for i in range(2)]
    arg = [at(f"arg{i}", [128, 4096], F32, 86016 + 16384 * i) for i in range(2)]
    outr = [at(f"outr{i}", [128, 1024], F32, 118784 + 4096 * i) for i in range(4)]

    wqkvT = at("wqkvT", [128, 24576], BF16, 152064)
    attn_cat = at("attn_cat", [128, 4096], BF16, 152064)
    attn_cat2 = at("attn_cat2", [128, 4096], BF16, 203264)
    p_sb0 = at("p_sb0", [128, 1024], BF16, 160256)
    p_sb1 = at("p_sb1", [128, 1024], BF16, 162304)
    rrow0 = at("rrow0", [1, 512], F32, 164352)
    rrow1 = at("rrow1", [1, 512], F32, 166400)
    rln0 = at("rln0", [1, 512], F32, 168448)
    rln1 = at("rln1", [1, 512], F32, 170496)
    rbr0 = at("rbr0", [1, 512], BF16, 180736)
    rbr1 = at("rbr1", [1, 512], BF16, 181760)
    rbsb0 = at("rbsb0", [64, 512], F32, 182784)
    rbsb1 = at("rbsb1", [64, 512], F32, 184832)
    proj0 = at("proj0", [128, 1024], F32, 172544)
    proj1 = at("proj1", [128, 1024], F32, 176640)

    xA = at("xA", [128, 16384], F32, 86016)
    accum = at("accum", [128, 16384], F32, 143360)

    assert base + 209408 <= nc.sbuf_top, (base, nc.sbuf_top)

    P01 = nc.alloc_psum_tensor("P01", [128, 1024], F32)
    P23 = nc.alloc_psum_tensor("P23", [128, 1024], F32)
    P45 = nc.alloc_psum_tensor("P45", [128, 1024], F32)
    P67 = nc.alloc_psum_tensor("P67", [128, 1024], F32)
    P4 = P45[:, 0:512]
    P5 = P45[:, 512:1024]
    P6 = P67[:, 0:512]
    P7 = P67[:, 512:1024]
    PH45 = [P4, P5]
    PH67 = [P6, P7]
    TRS = [P4, P5, P6, P7]

    ctx = contextlib.ExitStack()
    sem = {n: ctx.enter_context(nc.semaphore(n))
           for n in ("SX", "SQ0", "SQ1", "SW0", "SW1", "SW2", "SW3", "SXR", "SAR",
                     "SP0", "SP1", "SEA", "ST", "SV", "SS", "SO", "SG", "SCB0", "SCB1", "SCB2", "SCW", "SQW", "SQT")}
    C = {n: 0 for n in sem}
    P = _Plan()

    def W(eng, s, v):
        if v > 0:
            P.emit(eng, lambda e, s=s, v=v: e.wait_ge(sem[s], v))

    def Wtr(eng, entry):
        if isinstance(entry, tuple):
            for k in range(0, len(entry), 2):
                W(eng, entry[k], entry[k + 1])
        else:
            W(eng, "SV", entry)

    def I(eng, s, fn, n=1):
        if n >= 16 and C[s] > 0:
            W(eng, s, C[s])
        C[s] += n
        v = C[s]
        P.emit(eng, lambda e, fn=fn, s=s, n=n: fn(e).then_inc(sem[s], n))
        return v

    def E(eng, fn):
        P.emit(eng, fn)

    I("sync", "SX", lambda e: e.dma_start(
        wr_sb[:].rearrange("p (dc k) -> p dc k", k=4),
        wr_h.ap().rearrange("(dc p) k -> p dc k", p=128)), 16)
    I("sync", "SX", lambda e: e.dma_start(
        br_sb[0:1, 0:4], br_h.ap().rearrange("(a k) -> a k", a=1)), 16)
    I("sync", "SX", lambda e: e.dma_start(
        br_em[0:4, 0:1], br_h.ap().rearrange("(k a) -> k a", a=1)), 16)
    v_small = I("sync", "SX", lambda e: e.dma_start(
        b1st[:].rearrange("r (k c) -> r k c", k=4),
        b1_h.ap().rearrange("k (r c) -> r k c", r=32)), 16)

    def _consts(g):
        g.memset(ident[:], 0.0)
        g.drain()
        g.affine_select(out=ident[:], in_=ident[:], compare_op=ALU.not_equal,
                        fill=1.0, base=0, pattern=[[-1, 128]], channel_multiplier=1)
        g.memset(ones_f[:], 1.0)
        g.memset(ones_p[:], 1.0)
        g.memset(oh[:], 0.0)
        g.drain()
        for e_ in range(4):
            g.memset(oh[0:1, e_ * 4 + e_:e_ * 4 + e_ + 1], 1.0)
        g.memset(cneg8[:], -8.0)
        g.memset(cneg1[:], -1.0)
        g.memset(cm05[:], -0.5)
        g.memset(ones_bf[:], 1.0)
        g.memset(c0125[:], 0.125)
        g.memset(ceps[:], 1e-5)
        g.memset(masks[:], 1.0)
        g.drain()
        insts = None
        for j in range(4):
            insts = g.affine_select(
                out=masks[:, j * 512:(j + 1) * 512],
                in_=masks[:, j * 512:(j + 1) * 512],
                compare_op=ALU.is_ge, fill=0.0, base=-(128 * j),
                pattern=[[1, 512]], channel_multiplier=-1)
        return insts
    v_consts = I("gpsimd", "SG", _consts)

    W("scalar", "SG", v_consts)
    E("scalar", lambda s: s.activation(std_a[0:1, 0:1], ceps[0:1, 0:1], AF.Ln))
    E("scalar", lambda s: s.activation(r_a[0:1, 0:1], ceps[0:1, 0:1], AF.Exp))
    W("tensor", "SX", v_small)
    W("tensor", "SG", v_consts)
    v_b1c = [0, 0]
    for e_ in range(4):
        pt = PH45[e_ % 2]
        W("tensor", "SV", v_b1c[e_ % 2])
        v_tr = I("tensor", "ST", lambda t, e_=e_, pt=pt: t.transpose(
            pt[0:128, 0:32], b1st[0:32, e_ * 128:(e_ + 1) * 128], ident[0:32, 0:32]))
        W("vector", "ST", v_tr)
        v_b1c[e_ % 2] = I("vector", "SV", lambda v, e_=e_, pt=pt: v.tensor_copy(
            b1_sb[:, e_ * 32:(e_ + 1) * 32], pt[0:128, 0:32]))

    P.brk()
    W("gpsimd", "SXR", 16)
    W("gpsimd", "SAR", 16)
    for j, g_ in enumerate((1, 0, 2)):
        if j == 1:
            W("gpsimd", "SQT", 16 * 8)
        elif j == 2:
            W("gpsimd", "SQT", 16 * 16)
        I("gpsimd", "SCB%d" % g_, lambda g, g_=g_: g.dma_start(
            wqkvbf_h.ap()[g_ * 1024:(g_ + 1) * 1024, :],
            wqkv_h.ap()[g_ * 1024:(g_ + 1) * 1024, :]), 16)
    I("gpsimd", "SCW", lambda g: g.dma_start(woutbf_h.ap(), wout_h.ap()), 16)
    slots = [(P01, 0), (P01, 512), (P23, 0), (P23, 512)]
    slot_guard = [("SV", 0)] * 4
    slot_i = 0

    def emit_qk_tt(fc8, tt, wq_ev, xn_ev):
        nonlocal slot_i
        isQ = fc8 < 8
        Wtr("tensor", wq_ev)
        Wtr("tensor", xn_ev)
        pt, off = slots[slot_i % 4]
        gs, gv = slot_guard[slot_i % 4]
        W("tensor", gs, gv)
        for dc in range(NDC - 1):
            E("tensor", lambda t, fc8=fc8, tt=tt, dc=dc, pt=pt, off=off:
              t.matmul(pt[:, off:off + 512],
                       wqkvT[:, dc * 3072 + fc8 * 128:dc * 3072 + (fc8 + 1) * 128],
                       xn_fm[:, dc * 2048 + tt * 512:dc * 2048 + (tt + 1) * 512],
                       start=(dc == 0), stop=False))
        v_g = I("tensor", "ST", lambda t, fc8=fc8, tt=tt, pt=pt, off=off:
                t.matmul(pt[:, off:off + 512],
                         wqkvT[:, 7 * 3072 + fc8 * 128:7 * 3072 + (fc8 + 1) * 128],
                         xn_fm[:, 7 * 2048 + tt * 512:7 * 2048 + (tt + 1) * 512],
                         start=False, stop=True))
        if isQ:
            W("scalar", "ST", v_g)
            v_e = I("scalar", "SS", lambda s, fc8=fc8, tt=tt, pt=pt, off=off:
                    s.activation(
                        q_fm[:, fc8 * 2048 + tt * 512:fc8 * 2048 + (tt + 1) * 512],
                        pt[:, off:off + 512], AF.Identity, scale=c0125[:, 0:1]))
            slot_guard[slot_i % 4] = ("SS", v_e)
        else:
            W("vector", "ST", v_g)
            v_e = I("vector", "SV", lambda v, fc8=fc8, tt=tt, pt=pt, off=off:
                    v.tensor_copy(
                        k_fm[:, (fc8 - 8) * 2048 + tt * 512:(fc8 - 8) * 2048 + (tt + 1) * 512],
                        pt[:, off:off + 512]))
            slot_guard[slot_i % 4] = ("SV", v_e)
        slot_i += 1

    def emit_v(fh, ev):
        nonlocal slot_i
        Wtr("tensor", ev)
        for ts in range(NTS):
            pt, off = slots[slot_i % 4]
            gs, gv = slot_guard[slot_i % 4]
            W("tensor", gs, gv)
            for dc in range(NDC - 1):
                E("tensor", lambda t, ts=ts, fh=fh, dc=dc, pt=pt, off=off:
                  t.matmul(pt[:, off:off + 512],
                           xn_fm[:, dc * 2048 + ts * 128:dc * 2048 + (ts + 1) * 128],
                           wqkvT[:, dc * 3072 + 2048 + fh * 512:dc * 3072 + 2048 + (fh + 1) * 512],
                           start=(dc == 0), stop=False))
            v_g = I("tensor", "ST", lambda t, ts=ts, fh=fh, pt=pt, off=off:
                    t.matmul(pt[:, off:off + 512],
                             xn_fm[:, 7 * 2048 + ts * 128:7 * 2048 + (ts + 1) * 128],
                             wqkvT[:, 7 * 3072 + 2048 + fh * 512:7 * 3072 + 2048 + (fh + 1) * 512],
                             start=False, stop=True))
            W("vector", "ST", v_g)
            if fh == 0:
                E("vector", lambda v, ts=ts: v.memset(
                    bass.AP(v_ext, ts * VW + 64,
                            [[NTS * VW, 128], [65, 16], [1, 1]]), 1.0))
            v_e = I("vector", "SV", lambda v, ts=ts, fh=fh, pt=pt, off=off:
                    v.tensor_copy(
                        bass.AP(v_ext, ts * VW + fh * 8 * 65,
                                [[NTS * VW, 128], [65, 8], [1, 64]]),
                        pt[:, off:off + 512].rearrange("p (h j) -> p h j", j=64)))
            slot_guard[slot_i % 4] = ("SV", v_e)
            slot_i += 1


    tr_evac_chunk = [0] * 24
    v_xq = [0] * 4
    v_xh0 = [0, 0]
    qsem = ["SXR", "SAR", "SQ0", "SQ1"]
    for h_ in range(2):
        v_xh0[h_] = I("sync", "SXR", lambda e, h_=h_: e.dma_start(
            xA[:, h_ * 2048:(h_ + 1) * 2048].rearrange("p (ts j) -> p ts j", j=1024),
            x_h.ap()[h_ * 256:(h_ + 1) * 256, :]
            .rearrange("(ts p) j -> p ts j", p=128)), 16)
    v_xq[0] = v_xh0[1]
    for q_ in range(1, 4):
        v_xq[q_] = I("sync", qsem[q_], lambda e, q_=q_: e.dma_start(
            xA[:, q_ * 4096:(q_ + 1) * 4096].rearrange("p (ts j) -> p ts j", j=1024),
            x_h.ap()[q_ * 512:(q_ + 1) * 512, :]
            .rearrange("(ts p) j -> p ts j", p=128)), 16)

    for g_ in (1, 0, 2):
        W("sync", "SCB%d" % g_, 16)
        for fc in range(g_ * 8, (g_ + 1) * 8):
            v_tq = I("sync", "SQT", lambda e, fc=fc: e.dma_start_transpose(
                bass.AP(wqkvT, fc * 128, [[24576, 128], [3072, 8], [1, 128]]),
                wqkvbf_h.ap()[fc * 128:(fc + 1) * 128, :]), 16)
            tr_evac_chunk[fc] = ("SQT", v_tq)

    xsqQ = xsq0
    v_rstd = [0] * 4
    v_nmrq = [0] * 4
    tr_idx = 0
    tr_evac = {}
    v_xnf = [0] * NTS
    v_tr_last = [0] * NTS
    v_lgt = [0] * NTS
    xnf = [xnf0, xnf1]
    xsq = [xsq0] * 4
    v_xnq = [0] * 4
    v_ssq = [0] * 4
    qk_done = set()
    v_nmrh = {}

    def emit_ln(q_, half=None):
        ks = range(4) if half is None else range(half * 2, half * 2 + 2)
        if q_ != 0:
            W("vector", qsem[q_], v_xq[q_])
        elif half is not None:
            W("vector", "SXR", v_xh0[half])
        E("vector", lambda v: v.drain())
        for k_ in ks:
            ts = q_ * 4 + k_
            if k_ > ks.start:
                E("vector", lambda v: v.drain())
            E("vector", lambda v, ts=ts: v.tensor_tensor(
                xsq[ts % 4][:], xA[:, ts * 1024:(ts + 1) * 1024],
                xA[:, ts * 1024:(ts + 1) * 1024], ALU.mult))
            E("vector", lambda v, ts=ts: v.tensor_reduce(
                s1a[:, ts:ts + 1], xA[:, ts * 1024:(ts + 1) * 1024], AX.X, ALU.add))
            E("vector", lambda v: v.drain())
            E("vector", lambda v, ts=ts: v.tensor_reduce(
                s2a[:, ts:ts + 1], xsq[ts % 4][:], AX.X, ALU.add))
        sl = slice(q_ * 4 + ks.start, q_ * 4 + ks.stop)
        E("vector", lambda v: v.drain())
        E("vector", lambda v, sl=sl: v.tensor_scalar_mul(
            m_a[:, sl], s1a[:, sl], 1.0 / 1024.0))
        E("vector", lambda v, sl=sl: v.tensor_scalar_mul(
            E2a[:, sl], s2a[:, sl], 1.0 / 1024.0))
        E("vector", lambda v: v.drain())
        E("vector", lambda v, sl=sl: v.tensor_tensor(
            mma[:, sl], m_a[:, sl], m_a[:, sl], ALU.mult))
        E("vector", lambda v: v.drain())
        v_var = I("vector", "SV", lambda v, sl=sl: v.tensor_tensor(
            var_a[:, sl], E2a[:, sl], mma[:, sl], ALU.subtract))
        W("scalar", "SV", v_var)
        E("scalar", lambda s, sl=sl: s.activation(
            std_a[:, sl], var_a[:, sl], AF.Ln, bias=ceps[:, 0:1]))
        E("scalar", lambda s: s.drain())
        v_rstd[q_] = I("scalar", "SS", lambda s, sl=sl: s.activation(
            r_a[:, sl], std_a[:, sl], AF.Exp, scale=cm05[:, 0:1]))
        W("vector", "SS", v_rstd[q_])
        v_nmr = I("vector", "SV", lambda v, sl=sl: v.scalar_tensor_tensor(
            nmr_a[:, sl], m_a[:, sl], -1.0, r_a[:, sl], ALU.mult, ALU.mult))
        v_nmrq[q_] = v_nmr
        v_nmrh[(q_, half)] = v_nmr

    emit_ln(0, half=0)
    emit_ln(0, half=1)
    for q_ in range(4):
        for k_ in range(4):
            ts = q_ * 4 + k_
            if q_ == 0 and k_ == 0:
                W("scalar", "SV", v_nmrh[(0, 0)])
            elif q_ == 0 and k_ == 2:
                W("scalar", "SV", v_nmrh[(0, 1)])
            elif q_ != 0 and k_ == 0:
                W("scalar", "SV", v_nmrq[q_])
            if ts >= 2:
                W("scalar", "ST", v_tr_last[ts - 2])
            v_xnf[ts] = I("scalar", "SS", lambda s, ts=ts: s.activation(
                xnf[ts % 2][:], xA[:, ts * 1024:(ts + 1) * 1024], AF.Identity,
                bias=nmr_a[:, ts:ts + 1], scale=r_a[:, ts:ts + 1]))
            W("tensor", "SS", v_xnf[ts])
            v_ev_dc = 0
            for dc in range(NDC):
                trp = TRS[tr_idx % 4]
                if tr_idx >= 4:
                    Wtr("tensor", tr_evac[tr_idx - 4])
                v_t = I("tensor", "ST", lambda t, ts=ts, dc=dc, trp=trp: t.transpose(
                    trp[0:128, 0:128], xnf[ts % 2][:, dc * 128:(dc + 1) * 128],
                    ident[:]))
                v_tr_last[ts] = v_t
                if dc % 2 == 0:
                    W("scalar", "ST", v_t)
                    E("scalar", lambda s, ts=ts, dc=dc, trp=trp: s.activation(
                        xn_fm[:, dc * 2048 + ts * 128:dc * 2048 + (ts + 1) * 128],
                        trp[0:128, 0:128], AF.Identity))
                    v_ev1 = I("scalar", "SS", lambda s, k_=k_, dc=dc, trp=trp: s.activation(
                        xfm4[:, k_ * 1024 + dc * 128:k_ * 1024 + (dc + 1) * 128],
                        trp[0:128, 0:128], AF.Identity))
                    tr_evac[tr_idx] = ("SS", v_ev1)
                    v_ev_ss = v_ev1
                else:
                    W("vector", "ST", v_t)
                    E("vector", lambda v, ts=ts, dc=dc, trp=trp: v.tensor_copy(
                        xn_fm[:, dc * 2048 + ts * 128:dc * 2048 + (ts + 1) * 128],
                        trp[0:128, 0:128]))
                    v_ev1 = I("vector", "SV", lambda v, k_=k_, dc=dc, trp=trp: v.tensor_copy(
                        xfm4[:, k_ * 1024 + dc * 128:k_ * 1024 + (dc + 1) * 128],
                        trp[0:128, 0:128]))
                    tr_evac[tr_idx] = ("SV", v_ev1)
                    v_ev_sv = v_ev1
                tr_idx += 1
        v_xnq[q_] = ("SV", C["SV"])
        v_ssq[q_] = ("SS", C["SS"])
        trp = TRS[tr_idx % 4]
        Wtr("tensor", v_xnq[q_] + v_ssq[q_])
        if tr_idx >= 4:
            Wtr("tensor", tr_evac[tr_idx - 4])
        for dc in range(NDC - 1):
            E("tensor", lambda t, dc=dc, trp=trp: t.matmul(
                trp[0:4, 0:512], wr_sb[:, dc * 4:(dc + 1) * 4],
                bass.AP(xfm4, dc * 128, [[4096, 128], [1024, 4], [1, 128]]),
                start=(dc == 0), stop=False))
        v_rt = I("tensor", "ST", lambda t, trp=trp: t.matmul(
            trp[0:4, 0:512], wr_sb[:, 7 * 4:8 * 4],
            bass.AP(xfm4, 7 * 128, [[4096, 128], [1024, 4], [1, 128]]),
            start=False, stop=True))
        W("vector", "ST", v_rt)
        v_lem = I("vector", "SV", lambda v, trp=trp: v.tensor_scalar(
            lgt_em[:], trp[0:4, 0:512], br_em[0:4, 0:1], None, ALU.add))
        tr_evac[tr_idx] = v_lem
        tr_idx += 1
        for k_ in range(4):
            ts = q_ * 4 + k_
            trp = TRS[tr_idx % 4]
            W("tensor", "SV", v_lem)
            if tr_idx >= 4:
                Wtr("tensor", tr_evac[tr_idx - 4])
            v_t = I("tensor", "ST", lambda t, k_=k_, trp=trp: t.transpose(
                trp[0:128, 0:4], lgt_em[0:4, k_ * 128:(k_ + 1) * 128],
                ident[0:4, 0:4]))
            W("vector", "ST", v_t)
            v_lgt[ts] = I("vector", "SV", lambda v, ts=ts, trp=trp: v.tensor_copy(
                lgt[:, ts * 4:(ts + 1) * 4], trp[0:128, 0:4]))
            tr_evac[tr_idx] = v_lgt[ts]
            tr_idx += 1
        if q_ + 1 < 4:
            emit_ln(q_ + 1)
        for fc8 in range(8, 16):
            if tr_evac_chunk[fc8]:
                for tt in range(q_ + 1):
                    if (fc8, tt) not in qk_done and (fc8 - 8) // 4 <= q_:
                        emit_qk_tt(fc8, tt, tr_evac_chunk[fc8],
                                   v_xnq[tt] + v_xnq[(fc8 - 8) // 4] +
                                   v_ssq[tt] + v_ssq[(fc8 - 8) // 4])
                        qk_done.add((fc8, tt))
    v_xn_done = C["SV"]

    lgt3 = lgt[:].rearrange("p (ts k) -> p ts k", k=4)
    cmp3 = cmp[:].rearrange("p (i c) -> p i c", c=16)
    cnt3 = cnt[:].rearrange("p (ts k) -> p ts k", k=4)
    E("vector", lambda v: v.drain())
    for ts in range(NTS):
        E("vector", lambda v, ts=ts: v.tensor_reduce(
            m1a[:, ts:ts + 1], lgt[:, ts * 4:(ts + 1) * 4], AX.X, ALU.max))
    pairs = [(0, 1), (0, 2), (0, 3), (1, 2), (1, 3), (2, 3)]
    for q, (a_, b_) in enumerate(pairs):
        E("vector", lambda v, q=q, a_=a_, b_=b_: v.tensor_tensor(
            cmp3[:, 3 * q, :], lgt3[:, :, a_], lgt3[:, :, b_], ALU.is_gt))
        E("vector", lambda v, q=q, a_=a_, b_=b_: v.tensor_tensor(
            cmp3[:, 3 * q + 1, :], lgt3[:, :, a_], lgt3[:, :, b_], ALU.is_lt))
        E("vector", lambda v, q=q, a_=a_, b_=b_: v.tensor_tensor(
            cmp3[:, 3 * q + 2, :], lgt3[:, :, a_], lgt3[:, :, b_], ALU.is_equal))
    E("vector", lambda v: v.drain())
    terms = {0: [(0, 1), (1, 1), (2, 1)],
             1: [(0, 0), (3, 1), (4, 1), (0, 2)],
             2: [(1, 0), (3, 0), (5, 1), (1, 2), (3, 2)],
             3: [(2, 0), (4, 0), (5, 0), (2, 2), (4, 2), (5, 2)]}
    for e_, tl in terms.items():
        (q0, k0), (q1, k1) = tl[0], tl[1]
        E("vector", lambda v, e_=e_, q0=q0, k0=k0, q1=q1, k1=k1: v.tensor_tensor(
            cnt3[:, :, e_], cmp3[:, 3 * q0 + k0, :], cmp3[:, 3 * q1 + k1, :], ALU.add))
    E("vector", lambda v: v.drain())
    for lev in range(2, max(len(tl) for tl in terms.values())):
        for e_, tl in terms.items():
            if lev < len(tl):
                q_, k_ = tl[lev]
                E("vector", lambda v, e_=e_, q_=q_, k_=k_: v.tensor_tensor(
                    cnt3[:, :, e_], cnt3[:, :, e_], cmp3[:, 3 * q_ + k_, :], ALU.add))
        E("vector", lambda v: v.drain())
    E("vector", lambda v: v.tensor_single_scalar(mask2[:], cnt[:], 1.5, ALU.is_le))
    I("vector", "SV", lambda v: v.tensor_scalar_mul(nm1a[:], m1a[:], -1.0))
    E("vector", lambda v: v.drain())

    for fc8 in range(16):
        for tt in range(4):
            if (fc8, tt) not in qk_done:
                emit_qk_tt(fc8, tt, tr_evac_chunk[fc8],
                           v_xnq[3] + v_ssq[3])
                qk_done.add((fc8, tt))
    emit_v(0, tr_evac_chunk[19] + v_xnq[3] + v_ssq[3])
    emit_v(1, tr_evac_chunk[23] + v_xnq[3] + v_ssq[3])

    P.brk()
    W("sync", "SCW", 16)
    for fc in range(8):
        v_sqw = I("sync", "SQW", lambda e, fc=fc: e.dma_start_transpose(
            bass.AP(w_outT, fc * 128, [[8192, 128], [1024, 8], [1, 128]]),
            woutbf_h.ap()[fc * 128:(fc + 1) * 128, :]), 16)

    v_ex_last = 0
    for ts in range(NTS):
        v_ex_last = I("scalar", "SS", lambda s, ts=ts: s.activation(
            exa[:, ts * 4:(ts + 1) * 4], lgt[:, ts * 4:(ts + 1) * 4], AF.Exp,
            bias=nm1a[:, ts:ts + 1]))
    W("vector", "SS", v_ex_last)
    E("vector", lambda v: v.tensor_tensor(g2a[:], exa[:], mask2[:], ALU.mult))
    E("vector", lambda v: v.drain())
    for ts in range(NTS):
        E("vector", lambda v, ts=ts: v.tensor_reduce(
            Sa[:, ts:ts + 1], g2a[:, ts * 4:(ts + 1) * 4], AX.X, ALU.add))
    E("vector", lambda v: v.drain())
    E("vector", lambda v: v.reciprocal(rSa[:], Sa[:]))
    E("vector", lambda v: v.drain())
    v_ag = [0] * NTS
    for ts in range(NTS):
        v_ag[ts] = I("vector", "SV", lambda v, ts=ts: v.tensor_scalar_mul(
            agate[:, ts * 4:(ts + 1) * 4], g2a[:, ts * 4:(ts + 1) * 4],
            rSa[:, ts:ts + 1]))
    for ts in range(NTS):
        trp = TRS[tr_idx % 4]
        if tr_idx >= 4:
            Wtr("tensor", tr_evac[tr_idx - 4])
        W("tensor", "SV", v_ag[ts])
        v_t = I("tensor", "ST", lambda t, ts=ts, trp=trp: t.transpose(
            trp[0:4, 0:128], agate[:, ts * 4:(ts + 1) * 4], ident[:]))
        W("vector", "ST", v_t)
        v_ev = I("vector", "SV", lambda v, ts=ts, trp=trp: v.tensor_copy(
            aem[:, ts * 128:(ts + 1) * 128], trp[0:4, 0:128]))
        tr_evac[tr_idx] = v_ev
        tr_idx += 1

    w1g = [w1g0, w1g1]
    w2g = [w2g0, w2g1]
    h_sb = [h_sb0, h_sb1]

    def emit_w1_dma(i):
        e_, g_ = divmod(i, NG)
        buf = i % 2
        I("gpsimd", "SW%d" % buf, lambda g, e_=e_, g_=g_, buf=buf: g.dma_start(
            w1g[buf][:].rearrange("p (dc hh) -> p dc hh", hh=1024),
            w1_h.ap()[e_, :, g_ * 1024:(g_ + 1) * 1024]
            .rearrange("(dc p) hh -> p dc hh", p=128)), 16)

    def emit_w2_dma(i):
        e_, g_ = divmod(i, NG)
        buf = i % 2
        return I("gpsimd", "SW%d" % (2 + buf), lambda g, e_=e_, g_=g_, buf=buf: g.dma_start(
            w2g[buf][:].rearrange("p (hc d2) -> p hc d2", d2=1024),
            w2_h.ap()[e_, g_ * 1024:(g_ + 1) * 1024, :]
            .rearrange("(hc p) d2 -> p hc d2", p=128)), 16)

    def emit_slab_dma(i):
        emit_w1_dma(i)
        return emit_w2_dma(i)

    p_sb = [p_sb0, p_sb1]
    rrow = [rrow0, rrow1]
    rln = [rln0, rln1]
    rbr = [rbr0, rbr1]
    rbsb = [rbsb0, rbsb1]
    proj = [proj0, proj1]
    PB = [P45, P67]
    SLOT = [P23[:, 0:512], P23[:, 512:1024], P01[:, 0:512]]
    RBP = P01[:, 512:1024]
    sc_guard = [("SV", tr_evac[tr_idx - 1]), ("SV", tr_evac[tr_idx - 1])]
    pb_guard = [("ST", 0), ("ST", 0)]
    att_guard = [("SV", 0), ("SV", 0), ("SV", 0)]
    rb_guard = ("SS", 0)
    v_outproj = [0] * 4
    v_attnmult_last = [0]
    v_projev = [0, 0]
    v_projdma = [0, 0]

    def flush_stage1(pend):
        head, v_avl = pend[0], pend[4]
        sl = SLOT[slot_of[head]]
        W("vector", "ST", v_avl)
        return I("vector", "SV", lambda v, head=head, sl=sl: v.tensor_copy(
            rrow[head % 2][0:1, :], sl[64:65, :]))

    def flush_stage2(pend):
        nonlocal rb_guard
        head, pair, hr, qt_, v_r = pend[0], pend[1], pend[2], pend[3], pend[5]
        sl = SLOT[slot_of[head]]
        W("scalar", "SV", v_r)
        E("scalar", lambda s, head=head: s.activation(
            rln[head % 2][0:1, :], rrow[head % 2][0:1, :], AF.Ln))
        E("scalar", lambda s: s.drain())
        v_exp = I("scalar", "SS", lambda s, head=head: s.activation(
            rbr[head % 2][0:1, :], rln[head % 2][0:1, :], AF.Exp,
            scale=cneg1[0:1, 0:1]))
        gs, gv = rb_guard
        W("tensor", gs, gv)
        W("tensor", "SS", v_exp)
        v_rb = I("tensor", "ST", lambda t, head=head: t.matmul(
            RBP[0:64, :], ones_bf[0:1, 0:64],
            rbr[head % 2][0:1, :], start=True, stop=True))
        W("vector", "ST", v_rb)
        v_rc = I("vector", "SV", lambda v, head=head: v.tensor_copy(
            rbsb[head % 2][:], RBP[0:64, :]))
        rb_guard = ("SV", v_rc)
        E("vector", lambda v: v.drain())
        if qt_ >= 2:
            W("vector", "ST", v_outproj[qt_ - 2])
        ac = attn_cat if qt_ % 2 == 0 else attn_cat2
        v_am = I("vector", "SV", lambda v, head=head, pair=pair, hr=hr, sl=sl,
                 ac=ac: v.tensor_tensor(
                     ac[hr:hr + 64, pair * 512:(pair + 1) * 512],
                     sl[0:64, :], rbsb[head % 2][:], ALU.mult))
        att_guard[slot_of[head]] = ("SV", v_am)
        v_attnmult_last[0] = v_am

    def emit_pair(qt, head, p, npair, v_use, v_uses):
        bpar = head % 2
        pair = head // 2
        hr = (head % 2) * 64
        pb = PB[bpar]
        psb = p_sb[bpar]
        kb0 = 2 * p
        gs, gv = sc_guard[bpar]
        W("tensor", gs, gv)
        E("tensor", lambda t, kb0=kb0, pb=pb, hr=hr, pair=pair, qt=qt: t.matmul(
            pb[:, 0:512],
            k_fm[hr:hr + 64, pair * 2048 + kb0 * 128:pair * 2048 + (kb0 + 1) * 128],
            q_fm[hr:hr + 64, pair * 2048 + qt * 512:pair * 2048 + (qt + 1) * 512],
            start=True, stop=True))
        v_sc = I("tensor", "ST", lambda t, kb0=kb0, pb=pb, hr=hr, pair=pair, qt=qt:
                 t.matmul(
                     pb[:, 512:1024],
                     k_fm[hr:hr + 64, pair * 2048 + (kb0 + 1) * 128:pair * 2048 + (kb0 + 2) * 128],
                     q_fm[hr:hr + 64, pair * 2048 + qt * 512:pair * 2048 + (qt + 1) * 512],
                     start=True, stop=True))
        spg, spv = pb_guard[bpar]
        W("scalar", "ST", v_sc)
        W("scalar", spg, spv)
        ve = I("scalar", "SS", lambda s, pb=pb, psb=psb: s.activation(
            psb[:], pb[:, 0:1024], AF.Exp, bias=cneg8[:, 0:1]))
        sc_guard[bpar] = ("SS", ve)
        v_use[p] = ve
        v_uses[p] = "SS"
        if p >= npair - 2:
            j = p - (npair - 2)
            W("gpsimd", "SS", ve)
            vm = I("gpsimd", "SG", lambda g, psb=psb, j=j: g.tensor_tensor(
                psb[:], psb[:], masks[:, j * 1024:(j + 1) * 1024], ALU.mult))
            v_use[p] = vm
            v_uses[p] = "SG"

    def emit_av(qt, head, p, npair, v_use, v_uses, v_avp):
        bpar = head % 2
        psb = p_sb[bpar]
        sl = SLOT[slot_of[head]]
        W("tensor", v_uses[p], v_use[p])
        if p == 0:
            gs, gv = att_guard[slot_of[head]]
            W("tensor", gs, gv)
        E("tensor", lambda t, p=p, head=head, psb=psb, sl=sl: t.matmul(
            sl[0:65, :],
            bass.AP(v_ext, 2 * p * VW + head * 65,
                    [[NTS * VW, 128], [1, 65]]),
            psb[:, 0:512], start=(p == 0), stop=False))
        av_stop = (p == npair - 1)
        v_avp[p] = I("tensor", "ST", lambda t, p=p, head=head,
                     psb=psb, sl=sl, av_stop=av_stop: t.matmul(
                         sl[0:65, :],
                         bass.AP(v_ext, (2 * p + 1) * VW + head * 65,
                                 [[NTS * VW, 128], [1, 65]]),
                         psb[:, 512:1024], start=False, stop=av_stop))
        pb_guard[bpar] = ("ST", v_avp[p])

    pending = [None, None]
    op_defer = []
    slot_of = {h: h % 3 for h in range(16)}

    def emit_opuser(qt_src, ts4):
        bpar = ts4 % 2
        pb = PB[bpar]
        W("tensor", "SQW", v_sqw)
        gs, gv = sc_guard[bpar]
        W("tensor", gs, gv)
        if ts4 == 0:
            W("tensor", "SV", v_attnmult_last[0])
        ac = attn_cat if qt_src % 2 == 0 else attn_cat2
        for ec in range(NDC):
            for half in range(2):
                st = (ec == 0)
                sp = (ec == NDC - 1)
                if not (sp and half == 1):
                    E("tensor", lambda t, ts4=ts4, ec=ec, half=half, st=st,
                      sp=sp, pb=pb, ac=ac:
                      t.matmul(pb[:, half * 512:(half + 1) * 512],
                               ac[:, ec * 512 + ts4 * 128:ec * 512 + (ts4 + 1) * 128],
                               w_outT[:, ec * 1024 + half * 512:ec * 1024 + (half + 1) * 512],
                               start=st, stop=sp))
        v_op = I("tensor", "ST", lambda t, ts4=ts4, pb=pb, ac=ac:
                 t.matmul(pb[:, 512:1024],
                          ac[:, 7 * 512 + ts4 * 128:7 * 512 + (ts4 + 1) * 128],
                          w_outT[:, 7 * 1024 + 512:8 * 1024],
                          start=False, stop=True))
        v_outproj[qt_src] = v_op
        W("vector", "ST", v_op)
        if qt_src * 4 + ts4 >= 2:
            W("vector", "SP%d" % (ts4 % 2), v_projdma[ts4 % 2])
        v_pe = I("vector", "SV", lambda v, ts4=ts4, pb=pb: v.tensor_copy(
            proj[ts4 % 2][:], pb[:, 0:1024]))
        v_projev[ts4 % 2] = v_pe
        sc_guard[bpar] = ("SV", v_pe)
        W("sync", "SV", v_pe)
        gts = qt_src * 4 + ts4
        v_projdma[ts4 % 2] = I("sync", "SP%d" % (ts4 % 2), lambda e, gts=gts, ts4=ts4:
                               e.dma_start(
                                   attn_sc.ap()[gts * 128:(gts + 1) * 128, :],
                                   proj[ts4 % 2][:]), 16)

    for qt in range(4):
        kbmax = 4 * (qt + 1)
        npair = kbmax // 2
        hporder = [6, 7, 0, 1, 2, 3, 4, 5] if qt == 3 else list(range(NH // 2))
        for i_hp, hp in enumerate(hporder):
            h0 = 2 * hp
            h1 = 2 * hp + 1
            slot_of[h0] = (2 * i_hp) % 3
            slot_of[h1] = (2 * i_hp + 1) % 3
            u0, s0, a0 = [0] * npair, ["SS"] * npair, [0] * npair
            u1, s1_, a1 = [0] * npair, ["SS"] * npair, [0] * npair
            emit_pair(qt, h0, 0, npair, u0, s0)
            emit_pair(qt, h1, 0, npair, u1, s1_)
            for p in range(npair):
                emit_av(qt, h0, p, npair, u0, s0, a0)
                if p == 0 and pending[0] is not None:
                    flush_stage2(pending[0])
                if p + 1 < npair:
                    emit_pair(qt, h0, p + 1, npair, u0, s0)
                emit_av(qt, h1, p, npair, u1, s1_, a1)
                if p + 1 < npair:
                    emit_pair(qt, h1, p + 1, npair, u1, s1_)
                if op_defer and i_hp == 0:
                    op_defer.pop(0)()
            if pending[0] is not None:
                flush_stage2(pending[1])
            while op_defer and i_hp == 0:
                op_defer.pop(0)()
            pend0 = [h0, h0 // 2, (h0 % 2) * 64, qt, a0[npair - 1], 0]
            pend1 = [h1, h1 // 2, (h1 % 2) * 64, qt, a1[npair - 1], 0]
            pend0[5] = flush_stage1(pend0)
            pend1[5] = flush_stage1(pend1)
            pending = [pend0, pend1]
            if qt == 3 and i_hp == 1:
                W("gpsimd", "ST", C["ST"])
                I("gpsimd", "SW0", lambda g: g.dma_start(
                    w1g0[:, 0:4096].rearrange("p (dc hh) -> p dc hh", hh=1024),
                    w1_h.ap()[0, 0:512, 0:1024]
                    .rearrange("(dc p) hh -> p dc hh", p=128)), 16)
            if qt == 3 and i_hp == 3:
                W("gpsimd", "ST", C["ST"])
                I("gpsimd", "SW0", lambda g: g.dma_start(
                    w1g0[:, 4096:8192].rearrange("p (dc hh) -> p dc hh", hh=1024),
                    w1_h.ap()[0, 512:1024, 0:1024]
                    .rearrange("(dc p) hh -> p dc hh", p=128)), 16)
        if qt == 3:
            W("gpsimd", "ST", pending[1][4])
            emit_w2_dma(0)
            emit_w1_dma(1)
            emit_w2_dma(1)
        flush_stage2(pending[0])
        flush_stage2(pending[1])
        pending = [None, None]
        op_defer = [(lambda qt=qt, ts4=ts4: emit_opuser(qt, ts4)) for ts4 in range(4)]
        if qt == 3:
            for f in op_defer:
                f()
            op_defer = []

    P.brk()
    W("vector", "SP0", C["SP0"])
    W("vector", "SP1", C["SP1"])
    v_hfake = I("scalar", "SS", lambda s: s.activation(hfake[:], b1_sb[:], AF.Gelu))

    steps = [(i, tt) for i in range(16) for tt in range(NTT)]
    gh_idx = [0]
    gh_gelu = {}
    gy_idx = [0]
    gy_evic = {}
    hsb_lastread = {0: 0, 1: 0}
    v_lastuse = [0] * 16
    v_gevac = [0] * NE

    def emit_h(step):
        i, tt = steps[step]
        e_, g_ = divmod(i, NG)
        buf = i % 2
        if tt == 0:
            W("tensor", "SW%d" % (i % 2),
              16 * (i // 2 + 1) + (16 if i % 2 == 0 else 0))
        for hc in range(NHC):
            par = gh_idx[0] % 2
            W("tensor", "SS", gh_gelu.get(gh_idx[0] - 2, 0))
            v_g = I("tensor", "ST", lambda t, buf=buf, tt=tt, hc=hc, par=par: [
                t.matmul(PH67[par][:, 0:512],
                         w1g[buf][:, dc * 1024 + hc * 128:dc * 1024 + (hc + 1) * 128],
                         xn_fm[:, dc * 2048 + tt * 512:dc * 2048 + (tt + 1) * 512],
                         start=(dc == 0), stop=(dc == NDC - 1))
                for dc in range(NDC)][-1])
            W("scalar", "ST", v_g)
            W("scalar", "ST", hsb_lastread[tt % 2])
            vgl = I("scalar", "SS", lambda s, tt=tt, hc=hc, par=par, e_=e_, g_=g_:
                    s.activation(h_sb[tt % 2][:, hc * 512:(hc + 1) * 512],
                                 PH67[par][:, 0:512], AF.Gelu,
                                 bias=b1_sb[:, e_ * 32 + g_ * 8 + hc:e_ * 32 + g_ * 8 + hc + 1]))
            gh_gelu[gh_idx[0]] = vgl
            gh_idx[0] += 1

    def emit_y(step):
        i, tt = steps[step]
        e_, g_ = divmod(i, NG)
        buf = i % 2
        if tt == 0:
            W("tensor", "SW%d" % (2 + i % 2), 16 * (i // 2 + 1))
        W("tensor", "SS", gh_gelu[step * NHC + NHC - 1])
        for ts4 in range(4):
            tsg = tt * 4 + ts4
            pt = P01 if gy_idx[0] % 2 == 0 else P23
            W("tensor", "SV", gy_evic.get(gy_idx[0] - 2, 0))
            v_g = 0
            for hc in range(NHC):
                for half in range(2):
                    st = (hc == 0)
                    sp = (hc == NHC - 1)
                    fn = (lambda t, buf=buf, tt=tt, ts4=ts4, hc=hc, half=half,
                          pt=pt, st=st, sp=sp: t.matmul(
                              pt[:, half * 512:(half + 1) * 512],
                              h_sb[tt % 2][:, hc * 512 + ts4 * 128:hc * 512 + (ts4 + 1) * 128],
                              w2g[buf][:, hc * 1024 + half * 512:hc * 1024 + (half + 1) * 512],
                              start=st, stop=sp))
                    if sp and half == 1:
                        v_g = I("tensor", "ST", fn)
                    else:
                        E("tensor", fn)
            v_lastuse[i] = v_g
            hsb_lastread[tt % 2] = v_g
            W("vector", "ST", v_g)
            if i == 0:
                vev = I("vector", "SV", lambda v, tsg=tsg, e_=e_, pt=pt:
                        v.tensor_scalar_mul(
                            accum[:, tsg * 1024:(tsg + 1) * 1024], pt[:, 0:1024],
                            agate[:, tsg * 4 + e_:tsg * 4 + e_ + 1]))
            else:
                vev = I("vector", "SV", lambda v, tsg=tsg, e_=e_, pt=pt:
                        v.scalar_tensor_tensor(
                            accum[:, tsg * 1024:(tsg + 1) * 1024], pt[:, 0:1024],
                            agate[:, tsg * 4 + e_:tsg * 4 + e_ + 1],
                            accum[:, tsg * 1024:(tsg + 1) * 1024],
                            ALU.mult, ALU.add))
            gy_evic[gy_idx[0]] = vev
            gy_idx[0] += 1

    def emit_g(i):
        e_, g_ = divmod(i, NG)
        buf = i % 2
        W("tensor", "SW%d" % (2 + i % 2), 16 * (i // 2 + 1))
        if g_ == 0:
            if e_ > 0:
                W("tensor", "SV", v_gevac[e_ - 1])
            else:
                W("tensor", "SS", v_hfake)
        v_g = 0
        for hc in range(NHC):
            for half, pt in ((0, P4), (1, P5)):
                st = (g_ == 0 and hc == 0)
                sp = (g_ == NG - 1 and hc == NHC - 1)
                fn = (lambda t, buf=buf, e_=e_, g_=g_, hc=hc, pt=pt, half=half,
                      st=st, sp=sp: t.matmul(
                          pt[0:1, 0:512],
                          hfake[:, e_ * 32 + g_ * 8 + hc:e_ * 32 + g_ * 8 + hc + 1],
                          w2g[buf][:, hc * 1024 + half * 512:hc * 1024 + (half + 1) * 512],
                          start=st, stop=sp))
                if hc == NHC - 1 and half == 1:
                    v_g = I("tensor", "ST", fn)
                else:
                    E("tensor", fn)
        v_lastuse[i] = v_g
        if g_ == NG - 1:
            W("vector", "ST", v_g)
            E("vector", lambda v, e_=e_: v.tensor_copy(
                g_sb[0:1, e_ * 1024:e_ * 1024 + 512], P4[0:1, 0:512]))
            v_gevac[e_] = I("vector", "SV", lambda v, e_=e_: v.tensor_copy(
                g_sb[0:1, e_ * 1024 + 512:(e_ + 1) * 1024], P5[0:1, 0:512]))

    emit_h(0)
    for step in range(len(steps)):
        if step + 1 < len(steps):
            emit_h(step + 1)
        emit_y(step)
        i, tt = steps[step]
        if tt == NTT - 1:
            emit_g(i)
            if i + 2 < 16:
                W("gpsimd", "ST", v_lastuse[i])
                emit_slab_dma(i + 2)

    v_lasth = gh_gelu[gh_idx[0] - 1]
    W("sync", "SS", v_lasth)
    v_b2 = I("sync", "SX", lambda e: e.dma_start(b2_sb[:], b2_h.ap()), 16)
    W("tensor", "SV", v_gevac[NE - 1])
    W("tensor", "SV", gy_evic[gy_idx[0] - 2])
    W("tensor", "SV", gy_evic[gy_idx[0] - 1])
    v_g4 = 0
    for e_ in range(NE):
        for half in range(2):
            st = (e_ == 0)
            sp = (e_ == NE - 1)
            fn = (lambda t, e_=e_, half=half, st=st, sp=sp: t.matmul(
                P01[0:4, half * 512:(half + 1) * 512],
                oh[0:1, e_ * 4:(e_ + 1) * 4],
                g_sb[0:1, e_ * 1024 + half * 512:e_ * 1024 + (half + 1) * 512],
                start=st, stop=sp))
            if sp and half == 1:
                v_g4 = I("tensor", "ST", fn)
            else:
                E("tensor", fn)
    W("vector", "ST", v_g4)
    v_g4c = I("vector", "SV", lambda v: v.tensor_copy(g4_sb[:], P01[0:4, 0:1024]))
    E("vector", lambda v: v.drain())
    W("vector", "SX", v_b2)
    v_t4 = I("vector", "SV", lambda v: v.tensor_tensor(
        tmp4[:], g4_sb[:], b2_sb[:], ALU.add))
    W("tensor", "SV", v_t4)
    v_cs = 0
    for half in range(2):
        v_cs = I("tensor", "ST", lambda t, half=half: t.matmul(
            P23[0:1, half * 512:(half + 1) * 512], ones_p[0:4, 0:1],
            tmp4[0:4, half * 512:(half + 1) * 512], start=True, stop=True))
    W("vector", "ST", v_cs)
    v_csc = I("vector", "SV", lambda v: v.tensor_copy(csum_sb[:], P23[0:1, 0:1024]))
    W("tensor", "SV", v_csc)
    W("tensor", "SV", v_g4c)
    v_cb = 0
    for half in range(2):
        v_cb = I("tensor", "ST", lambda t, half=half: t.matmul(
            P01[0:4, half * 512:(half + 1) * 512], ones_f[0:1, 0:4],
            csum_sb[0:1, half * 512:(half + 1) * 512], start=True, stop=True))
    W("vector", "ST", v_cb)
    v_b2pv = I("vector", "SV", lambda v: v.tensor_tensor(
        b2p_sb[:], P01[0:4, 0:1024], g4_sb[:], ALU.subtract))
    E("vector", lambda v: v.drain())
    W("tensor", "SV", v_b2pv)
    v_bev = [0, 0, 0, 0]
    for ts in range(NTS):
        pt = [P01, P23, P45, P67][ts % 4]
        if ts >= 4:
            W("tensor", "SV", v_bev[ts % 4])
        v_bias = 0
        for half in range(2):
            v_bias = I("tensor", "ST", lambda t, ts=ts, half=half, pt=pt: t.matmul(
                pt[:, half * 512:(half + 1) * 512],
                aem[:, ts * 128:(ts + 1) * 128],
                b2p_sb[0:4, half * 512:(half + 1) * 512], start=True, stop=True))
        W("vector", "ST", v_bias)
        v_bev[ts % 4] = I("vector", "SV", lambda v, ts=ts, pt=pt: v.scalar_tensor_tensor(
            accum[:, ts * 1024:(ts + 1) * 1024], pt[:, 0:1024], 1.0,
            accum[:, ts * 1024:(ts + 1) * 1024], ALU.mult, ALU.add))

    P.brk()
    W("sync", "SP0", C["SP0"])
    W("sync", "SP1", C["SP1"])
    v_xg = [0] * 4
    v_ag = [0] * 4
    v_estt = [0] * NTS

    def emit_egrp(grp, guard):
        if guard:
            W("sync", "SV", guard)
        v_xg[grp] = I("sync", "SXR", lambda e, grp=grp: e.dma_start(
            xrg[grp % 2][:].rearrange("p (q j) -> p q j", j=1024),
            x_h.ap()[grp * 512:(grp + 1) * 512, :]
            .rearrange("(q p) j -> p q j", p=128)), 16)
        v_ag[grp] = I("sync", "SAR", lambda e, grp=grp: e.dma_start(
            arg[grp % 2][:].rearrange("p (q j) -> p q j", j=1024),
            attn_sc.ap()[grp * 512:(grp + 1) * 512, :]
            .rearrange("(q p) j -> p q j", p=128)), 16)

    emit_egrp(0, 0)
    emit_egrp(1, 0)
    for ts in range(NTS):
        grp = ts // 4
        q4 = ts % 4
        W("vector", "SXR", v_xg[grp])
        W("vector", "SAR", v_ag[grp])
        if ts >= 4:
            W("vector", "SO", (ts - 4 + 1) * 16)
        E("vector", lambda v, ts=ts, grp=grp, q4=q4: v.tensor_tensor(
            outr[ts % 4][:], xrg[grp % 2][:, q4 * 1024:(q4 + 1) * 1024],
            arg[grp % 2][:, q4 * 1024:(q4 + 1) * 1024], ALU.add))
        E("vector", lambda v: v.drain())
        v_estt[ts] = I("vector", "SV", lambda v, ts=ts: v.tensor_tensor(
            outr[ts % 4][:], outr[ts % 4][:],
            accum[:, ts * 1024:(ts + 1) * 1024], ALU.add))
        W("sync", "SV", v_estt[ts])
        I("sync", "SO", lambda e, ts=ts: e.dma_start(
            y_h.ap()[ts * 128:(ts + 1) * 128, :], outr[ts % 4][:]), 16)
        if ts == 3:
            emit_egrp(2, v_estt[3])
        elif ts == 7:
            emit_egrp(3, v_estt[7])
    W("sync", "SO", 16 * NTS)

    P.run(nc)
    return nc, ctx


_CACHE = {}


def kernel(L0_attn_x, L0_attn_w_qkv, L0_attn_w_out, L0_ffn_w1, L0_ffn_b1,
           L0_ffn_w2, L0_ffn_b2, L0_ffn_wr, L0_ffn_br):
    if "nc" not in _CACHE:
        _CACHE["nc"] = build_nc()
    nc, _ctx = _CACHE["nc"]

    x = np.ascontiguousarray(np.asarray(L0_attn_x, dtype=np.float32))
    wqkv = np.ascontiguousarray(np.asarray(L0_attn_w_qkv, dtype=np.float32))
    wout = np.ascontiguousarray(np.asarray(L0_attn_w_out, dtype=np.float32))
    w1 = np.ascontiguousarray(np.asarray(L0_ffn_w1, dtype=np.float32))
    b1 = np.ascontiguousarray(np.asarray(L0_ffn_b1, dtype=np.float32))
    w2 = np.ascontiguousarray(np.asarray(L0_ffn_w2, dtype=np.float32))
    b2 = np.ascontiguousarray(np.asarray(L0_ffn_b2, dtype=np.float32))
    wr = np.ascontiguousarray(np.asarray(L0_ffn_wr, dtype=np.float32))
    br = np.ascontiguousarray(np.asarray(L0_ffn_br, dtype=np.float32))

    in_maps = [{"x": x[c], "w_qkv": wqkv, "w_out": wout, "w1": w1, "b1": b1,
                "w2": w2, "b2": b2, "wr": wr, "br": br} for c in range(8)]
    res = run_bass_kernel_spmd(nc, in_maps, core_ids=list(range(8)))
    return np.stack([np.asarray(r["y"]) for r in res.results], axis=0)


# version 36
# speedup vs baseline: 1.0343x; 1.0343x over previous
import contextlib
import numpy as np

import concourse.bass as bass
import concourse.mybir as mybir
from concourse.bass_utils import run_bass_kernel_spmd

F32 = mybir.dt.float32
BF16 = mybir.dt.bfloat16
AF = mybir.ActivationFunctionType
ALU = mybir.AluOpType
AX = mybir.AxisListType

S = 2048
D = 1024
NH = 16
NE = 4
HID = 4096
NTS = 16
NTT = 4
NDC = 8
NG = 4
NHC = 8
VW = NH * 65

ENGINES = ("sync", "gpsimd", "tensor", "vector", "scalar")


class _Plan:
    def __init__(self):
        self.blocks = [{e: [] for e in ENGINES}]

    def brk(self):
        self.blocks.append({e: [] for e in ENGINES})

    def emit(self, eng, fn):
        self.blocks[-1][eng].append(fn)

    def run(self, nc):
        for blk_ops in self.blocks:
            if not any(blk_ops[e] for e in ENGINES):
                continue
            with nc.Block() as blk:
                for eng in ENGINES:
                    ops = blk_ops[eng]
                    if ops:
                        getattr(blk, eng)(lambda e, ops=ops: [f(e) for f in ops])


def build_nc():
    nc = bass.Bass("TRN2", target_bir_lowering=False)

    x_h = nc.dram_tensor("x", [S, D], F32, kind="ExternalInput")
    wqkv_h = nc.dram_tensor("w_qkv", [3 * D, D], F32, kind="ExternalInput")
    wout_h = nc.dram_tensor("w_out", [D, D], F32, kind="ExternalInput")
    w1_h = nc.dram_tensor("w1", [NE, D, HID], F32, kind="ExternalInput")
    b1_h = nc.dram_tensor("b1", [NE, HID], F32, kind="ExternalInput")
    w2_h = nc.dram_tensor("w2", [NE, HID, D], F32, kind="ExternalInput")
    b2_h = nc.dram_tensor("b2", [NE, D], F32, kind="ExternalInput")
    wr_h = nc.dram_tensor("wr", [D, NE], F32, kind="ExternalInput")
    br_h = nc.dram_tensor("br", [NE], F32, kind="ExternalInput")
    y_h = nc.dram_tensor("y", [S, D], F32, kind="ExternalOutput")
    attn_sc = nc.dram_tensor("attn_sc", [S, D], F32, kind="Internal")
    wqkvbf_h = nc.dram_tensor("wqkv_bf", [3 * D, D], BF16, kind="Internal")
    woutbf_h = nc.dram_tensor("wout_bf", [D, D], BF16, kind="Internal")

    base = (nc.sbuf_base + 31) // 32 * 32

    def at(name, shape, dt, off):
        return nc.alloc_sbuf_tensor_at(name, shape, dt, offset=base + off)

    ident = at("ident", [128, 128], F32, 0)
    ones_f = at("ones_f", [1, 128], F32, 512)
    ones_p = at("ones_p", [128, 1], F32, 1024)
    oh = at("oh", [1, 16], BF16, 1056)
    cneg8 = at("cneg8", [128, 1], F32, 1120)
    c0125 = at("c0125", [128, 1], F32, 1152)
    ceps = at("ceps", [128, 1], F32, 1184)
    wr_sb = at("wr_sb", [128, 32], F32, 1216)
    br_sb = at("br_sb", [1, 4], F32, 1344)
    b1_sb = at("b1_sb", [128, 128], F32, 1376)
    hfake = at("hfake", [128, 128], BF16, 1888)
    s1a = at("s1a", [128, 16], F32, 2400)
    s2a = at("s2a", [128, 16], F32, 2464)
    m_a = at("m_a", [128, 16], F32, 2528)
    E2a = at("E2a", [128, 16], F32, 2592)
    mma = at("mma", [128, 16], F32, 2656)
    var_a = at("var_a", [128, 16], F32, 2720)
    std_a = at("std_a", [128, 16], F32, 2784)
    r_a = at("r_a", [128, 16], F32, 2848)
    nmr_a = at("nmr_a", [128, 16], F32, 2912)
    lgt = at("lgt", [128, 64], F32, 2976)
    br_em = at("br_em", [4, 1], F32, 1088)
    lgt_em = at("lgt_em", [4, 512], F32, 18208)
    xfm4 = at("xfm4", [128, 4096], F32, 57344)
    m1a = at("m1a", [128, 16], F32, 3232)
    nm1a = at("nm1a", [128, 16], F32, 3296)
    cmp = at("cmp", [128, 288], F32, 3360)
    cnt = at("cnt", [128, 64], F32, 4512)
    mask2 = at("mask2", [128, 64], F32, 4768)
    exa = at("exa", [128, 64], F32, 5024)
    g2a = at("g2a", [128, 64], F32, 5280)
    Sa = at("Sa", [128, 16], F32, 5536)
    rSa = at("rSa", [128, 16], F32, 5600)
    agate = at("agate", [128, 64], F32, 5664)
    aem = at("aem", [4, 2048], F32, 5920)
    masks = at("masks", [128, 2048], BF16, 14112)
    b1st = at("b1st", [32, 512], F32, 18208)
    cneg1 = at("cneg1", [128, 1], F32, 20256)
    ones_bf = at("ones_bf", [1, 64], BF16, 20288)
    cm05 = at("cm05", [128, 1], F32, 20416)

    xn_fm = at("xn_fm", [128, NDC * S], BF16, 20480)
    w_outT = at("w_outT", [128, 8192], BF16, 186880)
    b2_sb = at("b2_sb", [4, 1024], F32, 20480)
    g4_sb = at("g4_sb", [4, 1024], F32, 24576)
    tmp4 = at("tmp4", [4, 1024], F32, 28672)
    csum_sb = at("csum_sb", [1, 1024], F32, 32768)
    b2p_sb = at("b2p_sb", [4, 1024], F32, 36864)

    q_fm = at("q_fm", [128, 16384], BF16, 53248)
    k_fm = at("k_fm", [128, 16384], BF16, 86016)
    v_ext = at("v_ext", [128, NTS * VW], BF16, 118784)
    xsq0 = at("xsq0", [128, 1024], F32, 53248)
    xsq1 = at("xsq1", [128, 1024], F32, 57344)
    xsq2 = at("xsq2", [128, 1024], F32, 61440)
    xsq3 = at("xsq3", [128, 1024], F32, 65536)
    xfm = at("xfm", [128, 1024], F32, 69632)
    xnf0 = at("xnf0", [128, 1024], F32, 73728)
    xnf1 = at("xnf1", [128, 1024], F32, 77824)
    h_sb0 = at("h_sb0", [128, 4096], BF16, 53248)
    h_sb1 = at("h_sb1", [128, 4096], BF16, 61440)
    g_sb = at("g_sb", [1, 4096], BF16, 69632)
    w1g0 = at("w1g0", [128, 8192], BF16, 77824)
    w1g1 = at("w1g1", [128, 8192], BF16, 94208)
    w2g0 = at("w2g0", [128, 8192], BF16, 110592)
    w2g1 = at("w2g1", [128, 8192], BF16, 126976)
    xrg = [at(f"xrg{i}", [128, 4096], F32, 53248 + 16384 * i) for i in range(2)]
    arg = [at(f"arg{i}", [128, 4096], F32, 86016 + 16384 * i) for i in range(2)]
    outr = [at(f"outr{i}", [128, 1024], F32, 118784 + 4096 * i) for i in range(4)]

    wqkvT = at("wqkvT", [128, 24576], BF16, 152064)
    attn_cat = at("attn_cat", [128, 4096], BF16, 152064)
    attn_cat2 = at("attn_cat2", [128, 4096], BF16, 203264)
    p_sb0 = at("p_sb0", [128, 1024], BF16, 160256)
    p_sb1 = at("p_sb1", [128, 1024], BF16, 162304)
    rrow0 = at("rrow0", [1, 512], F32, 164352)
    rrow1 = at("rrow1", [1, 512], F32, 166400)
    rln0 = at("rln0", [1, 512], F32, 168448)
    rln1 = at("rln1", [1, 512], F32, 170496)
    rbr0 = at("rbr0", [1, 512], BF16, 180736)
    rbr1 = at("rbr1", [1, 512], BF16, 181760)
    rbsb0 = at("rbsb0", [64, 512], F32, 182784)
    rbsb1 = at("rbsb1", [64, 512], F32, 184832)
    proj0 = at("proj0", [128, 1024], F32, 172544)
    proj1 = at("proj1", [128, 1024], F32, 176640)

    xA = at("xA", [128, 16384], F32, 86016)
    accum = at("accum", [128, 16384], F32, 143360)

    assert base + 209408 <= nc.sbuf_top, (base, nc.sbuf_top)

    P01 = nc.alloc_psum_tensor("P01", [128, 1024], F32)
    P23 = nc.alloc_psum_tensor("P23", [128, 1024], F32)
    P45 = nc.alloc_psum_tensor("P45", [128, 1024], F32)
    P67 = nc.alloc_psum_tensor("P67", [128, 1024], F32)
    P4 = P45[:, 0:512]
    P5 = P45[:, 512:1024]
    P6 = P67[:, 0:512]
    P7 = P67[:, 512:1024]
    PH45 = [P4, P5]
    PH67 = [P6, P7]
    TRS = [P4, P5, P6, P7]

    ctx = contextlib.ExitStack()
    sem = {n: ctx.enter_context(nc.semaphore(n))
           for n in ("SX", "SQ0", "SQ1", "SW0", "SW1", "SW2", "SW3", "SXR", "SAR",
                     "SP0", "SP1", "SEA", "ST", "SV", "SS", "SO", "SG", "SCB0", "SCB1", "SCB2", "SCW", "SQW", "SQT")}
    C = {n: 0 for n in sem}
    P = _Plan()

    def W(eng, s, v):
        if v > 0:
            P.emit(eng, lambda e, s=s, v=v: e.wait_ge(sem[s], v))

    def Wtr(eng, entry):
        if isinstance(entry, tuple):
            for k in range(0, len(entry), 2):
                W(eng, entry[k], entry[k + 1])
        else:
            W(eng, "SV", entry)

    def I(eng, s, fn, n=1):
        if n >= 16 and C[s] > 0:
            W(eng, s, C[s])
        C[s] += n
        v = C[s]
        P.emit(eng, lambda e, fn=fn, s=s, n=n: fn(e).then_inc(sem[s], n))
        return v

    def E(eng, fn):
        P.emit(eng, fn)

    I("sync", "SX", lambda e: e.dma_start(
        wr_sb[:].rearrange("p (dc k) -> p dc k", k=4),
        wr_h.ap().rearrange("(dc p) k -> p dc k", p=128)), 16)
    I("sync", "SX", lambda e: e.dma_start(
        br_sb[0:1, 0:4], br_h.ap().rearrange("(a k) -> a k", a=1)), 16)
    I("sync", "SX", lambda e: e.dma_start(
        br_em[0:4, 0:1], br_h.ap().rearrange("(k a) -> k a", a=1)), 16)
    v_small = I("sync", "SX", lambda e: e.dma_start(
        b1st[:].rearrange("r (k c) -> r k c", k=4),
        b1_h.ap().rearrange("k (r c) -> r k c", r=32)), 16)

    def _consts(g):
        g.memset(ident[:], 0.0)
        g.drain()
        g.affine_select(out=ident[:], in_=ident[:], compare_op=ALU.not_equal,
                        fill=1.0, base=0, pattern=[[-1, 128]], channel_multiplier=1)
        g.memset(ones_f[:], 1.0)
        g.memset(ones_p[:], 1.0)
        g.memset(oh[:], 0.0)
        g.drain()
        for e_ in range(4):
            g.memset(oh[0:1, e_ * 4 + e_:e_ * 4 + e_ + 1], 1.0)
        g.memset(cneg8[:], -8.0)
        g.memset(cneg1[:], -1.0)
        g.memset(cm05[:], -0.5)
        g.memset(ones_bf[:], 1.0)
        g.memset(c0125[:], 0.125)
        g.memset(ceps[:], 1e-5)
        g.memset(masks[:], 1.0)
        g.drain()
        insts = None
        for j in range(4):
            insts = g.affine_select(
                out=masks[:, j * 512:(j + 1) * 512],
                in_=masks[:, j * 512:(j + 1) * 512],
                compare_op=ALU.is_ge, fill=0.0, base=-(128 * j),
                pattern=[[1, 512]], channel_multiplier=-1)
        return insts
    v_consts = I("gpsimd", "SG", _consts)

    W("scalar", "SG", v_consts)
    E("scalar", lambda s: s.activation(std_a[0:1, 0:1], ceps[0:1, 0:1], AF.Ln))
    E("scalar", lambda s: s.activation(r_a[0:1, 0:1], ceps[0:1, 0:1], AF.Exp))
    W("tensor", "SX", v_small)
    W("tensor", "SG", v_consts)
    v_b1c = [0, 0]
    for e_ in range(4):
        pt = PH45[e_ % 2]
        W("tensor", "SV", v_b1c[e_ % 2])
        v_tr = I("tensor", "ST", lambda t, e_=e_, pt=pt: t.transpose(
            pt[0:128, 0:32], b1st[0:32, e_ * 128:(e_ + 1) * 128], ident[0:32, 0:32]))
        W("vector", "ST", v_tr)
        v_b1c[e_ % 2] = I("vector", "SV", lambda v, e_=e_, pt=pt: v.tensor_copy(
            b1_sb[:, e_ * 32:(e_ + 1) * 32], pt[0:128, 0:32]))

    P.brk()
    W("gpsimd", "SXR", 16)
    W("gpsimd", "SAR", 16)
    for j, g_ in enumerate((1, 0, 2)):
        if j == 1:
            W("gpsimd", "SQT", 16 * 8)
        elif j == 2:
            W("gpsimd", "SQT", 16 * 16)
        I("gpsimd", "SCB%d" % g_, lambda g, g_=g_: g.dma_start(
            wqkvbf_h.ap()[g_ * 1024:(g_ + 1) * 1024, :],
            wqkv_h.ap()[g_ * 1024:(g_ + 1) * 1024, :]), 16)
    I("gpsimd", "SCW", lambda g: g.dma_start(woutbf_h.ap(), wout_h.ap()), 16)
    slots = [(P01, 0), (P01, 512), (P23, 0), (P23, 512)]
    slot_guard = [("SV", 0)] * 4
    slot_i = 0

    def emit_qk_tt(fc8, tt, wq_ev, xn_ev):
        nonlocal slot_i
        isQ = fc8 < 8
        Wtr("tensor", wq_ev)
        Wtr("tensor", xn_ev)
        pt, off = slots[slot_i % 4]
        gs, gv = slot_guard[slot_i % 4]
        W("tensor", gs, gv)
        for dc in range(NDC - 1):
            E("tensor", lambda t, fc8=fc8, tt=tt, dc=dc, pt=pt, off=off:
              t.matmul(pt[:, off:off + 512],
                       wqkvT[:, dc * 3072 + fc8 * 128:dc * 3072 + (fc8 + 1) * 128],
                       xn_fm[:, dc * 2048 + tt * 512:dc * 2048 + (tt + 1) * 512],
                       start=(dc == 0), stop=False))
        v_g = I("tensor", "ST", lambda t, fc8=fc8, tt=tt, pt=pt, off=off:
                t.matmul(pt[:, off:off + 512],
                         wqkvT[:, 7 * 3072 + fc8 * 128:7 * 3072 + (fc8 + 1) * 128],
                         xn_fm[:, 7 * 2048 + tt * 512:7 * 2048 + (tt + 1) * 512],
                         start=False, stop=True))
        if isQ:
            W("scalar", "ST", v_g)
            v_e = I("scalar", "SS", lambda s, fc8=fc8, tt=tt, pt=pt, off=off:
                    s.activation(
                        q_fm[:, fc8 * 2048 + tt * 512:fc8 * 2048 + (tt + 1) * 512],
                        pt[:, off:off + 512], AF.Identity, scale=c0125[:, 0:1]))
            slot_guard[slot_i % 4] = ("SS", v_e)
        else:
            W("vector", "ST", v_g)
            v_e = I("vector", "SV", lambda v, fc8=fc8, tt=tt, pt=pt, off=off:
                    v.tensor_copy(
                        k_fm[:, (fc8 - 8) * 2048 + tt * 512:(fc8 - 8) * 2048 + (tt + 1) * 512],
                        pt[:, off:off + 512]))
            slot_guard[slot_i % 4] = ("SV", v_e)
        slot_i += 1

    def emit_v(fh, ev):
        nonlocal slot_i
        Wtr("tensor", ev)
        for ts in range(NTS):
            pt, off = slots[slot_i % 4]
            gs, gv = slot_guard[slot_i % 4]
            W("tensor", gs, gv)
            for dc in range(NDC - 1):
                E("tensor", lambda t, ts=ts, fh=fh, dc=dc, pt=pt, off=off:
                  t.matmul(pt[:, off:off + 512],
                           xn_fm[:, dc * 2048 + ts * 128:dc * 2048 + (ts + 1) * 128],
                           wqkvT[:, dc * 3072 + 2048 + fh * 512:dc * 3072 + 2048 + (fh + 1) * 512],
                           start=(dc == 0), stop=False))
            v_g = I("tensor", "ST", lambda t, ts=ts, fh=fh, pt=pt, off=off:
                    t.matmul(pt[:, off:off + 512],
                             xn_fm[:, 7 * 2048 + ts * 128:7 * 2048 + (ts + 1) * 128],
                             wqkvT[:, 7 * 3072 + 2048 + fh * 512:7 * 3072 + 2048 + (fh + 1) * 512],
                             start=False, stop=True))
            W("vector", "ST", v_g)
            if fh == 0:
                E("vector", lambda v, ts=ts: v.memset(
                    bass.AP(v_ext, ts * VW + 64,
                            [[NTS * VW, 128], [65, 16], [1, 1]]), 1.0))
            v_e = I("vector", "SV", lambda v, ts=ts, fh=fh, pt=pt, off=off:
                    v.tensor_copy(
                        bass.AP(v_ext, ts * VW + fh * 8 * 65,
                                [[NTS * VW, 128], [65, 8], [1, 64]]),
                        pt[:, off:off + 512].rearrange("p (h j) -> p h j", j=64)))
            slot_guard[slot_i % 4] = ("SV", v_e)
            slot_i += 1


    tr_evac_chunk = [0] * 24
    v_xq = [0] * 4
    v_xh0 = [0, 0]
    qsem = ["SXR", "SAR", "SQ0", "SQ1"]
    for h_ in range(2):
        v_xh0[h_] = I("sync", "SXR", lambda e, h_=h_: e.dma_start(
            xA[:, h_ * 2048:(h_ + 1) * 2048].rearrange("p (ts j) -> p ts j", j=1024),
            x_h.ap()[h_ * 256:(h_ + 1) * 256, :]
            .rearrange("(ts p) j -> p ts j", p=128)), 16)
    v_xq[0] = v_xh0[1]
    for q_ in range(1, 4):
        v_xq[q_] = I("sync", qsem[q_], lambda e, q_=q_: e.dma_start(
            xA[:, q_ * 4096:(q_ + 1) * 4096].rearrange("p (ts j) -> p ts j", j=1024),
            x_h.ap()[q_ * 512:(q_ + 1) * 512, :]
            .rearrange("(ts p) j -> p ts j", p=128)), 16)

    for g_ in (1, 0, 2):
        W("sync", "SCB%d" % g_, 16)
        for fc in range(g_ * 8, (g_ + 1) * 8):
            v_tq = I("sync", "SQT", lambda e, fc=fc: e.dma_start_transpose(
                bass.AP(wqkvT, fc * 128, [[24576, 128], [3072, 8], [1, 128]]),
                wqkvbf_h.ap()[fc * 128:(fc + 1) * 128, :]), 16)
            tr_evac_chunk[fc] = ("SQT", v_tq)

    xsqQ = xsq0
    v_rstd = [0] * 4
    v_nmrq = [0] * 4
    tr_idx = 0
    tr_evac = {}
    v_xnf = [0] * NTS
    v_tr_last = [0] * NTS
    v_lgt = [0] * NTS
    xnf = [xnf0, xnf1]
    xsq = [xsq0] * 4
    v_xnq = [0] * 4
    v_ssq = [0] * 4
    qk_done = set()
    v_nmrh = {}

    def emit_ln(q_, half=None):
        ks = range(4) if half is None else range(half * 2, half * 2 + 2)
        if q_ != 0:
            W("vector", qsem[q_], v_xq[q_])
        elif half is not None:
            W("vector", "SXR", v_xh0[half])
        E("vector", lambda v: v.drain())
        for k_ in ks:
            ts = q_ * 4 + k_
            if k_ > ks.start:
                E("vector", lambda v: v.drain())
            E("vector", lambda v, ts=ts: v.tensor_tensor(
                xsq[ts % 4][:], xA[:, ts * 1024:(ts + 1) * 1024],
                xA[:, ts * 1024:(ts + 1) * 1024], ALU.mult))
            E("vector", lambda v, ts=ts: v.tensor_reduce(
                s1a[:, ts:ts + 1], xA[:, ts * 1024:(ts + 1) * 1024], AX.X, ALU.add))
            E("vector", lambda v: v.drain())
            E("vector", lambda v, ts=ts: v.tensor_reduce(
                s2a[:, ts:ts + 1], xsq[ts % 4][:], AX.X, ALU.add))
        sl = slice(q_ * 4 + ks.start, q_ * 4 + ks.stop)
        E("vector", lambda v: v.drain())
        E("vector", lambda v, sl=sl: v.tensor_scalar_mul(
            m_a[:, sl], s1a[:, sl], 1.0 / 1024.0))
        E("vector", lambda v, sl=sl: v.tensor_scalar_mul(
            E2a[:, sl], s2a[:, sl], 1.0 / 1024.0))
        E("vector", lambda v: v.drain())
        E("vector", lambda v, sl=sl: v.tensor_tensor(
            mma[:, sl], m_a[:, sl], m_a[:, sl], ALU.mult))
        E("vector", lambda v: v.drain())
        v_var = I("vector", "SV", lambda v, sl=sl: v.tensor_tensor(
            var_a[:, sl], E2a[:, sl], mma[:, sl], ALU.subtract))
        W("scalar", "SV", v_var)
        E("scalar", lambda s, sl=sl: s.activation(
            std_a[:, sl], var_a[:, sl], AF.Ln, bias=ceps[:, 0:1]))
        E("scalar", lambda s: s.drain())
        v_rstd[q_] = I("scalar", "SS", lambda s, sl=sl: s.activation(
            r_a[:, sl], std_a[:, sl], AF.Exp, scale=cm05[:, 0:1]))
        W("vector", "SS", v_rstd[q_])
        v_nmr = I("vector", "SV", lambda v, sl=sl: v.scalar_tensor_tensor(
            nmr_a[:, sl], m_a[:, sl], -1.0, r_a[:, sl], ALU.mult, ALU.mult))
        v_nmrq[q_] = v_nmr
        v_nmrh[(q_, half)] = v_nmr

    emit_ln(0, half=0)
    emit_ln(0, half=1)
    for q_ in range(4):
        for k_ in range(4):
            ts = q_ * 4 + k_
            if q_ == 0 and k_ == 0:
                W("scalar", "SV", v_nmrh[(0, 0)])
            elif q_ == 0 and k_ == 2:
                W("scalar", "SV", v_nmrh[(0, 1)])
            elif q_ != 0 and k_ == 0:
                W("scalar", "SV", v_nmrq[q_])
            if ts >= 2:
                W("scalar", "ST", v_tr_last[ts - 2])
            v_xnf[ts] = I("scalar", "SS", lambda s, ts=ts: s.activation(
                xnf[ts % 2][:], xA[:, ts * 1024:(ts + 1) * 1024], AF.Identity,
                bias=nmr_a[:, ts:ts + 1], scale=r_a[:, ts:ts + 1]))
            W("tensor", "SS", v_xnf[ts])
            v_ev_dc = 0
            for dc in range(NDC):
                trp = TRS[tr_idx % 4]
                if tr_idx >= 4:
                    Wtr("tensor", tr_evac[tr_idx - 4])
                v_t = I("tensor", "ST", lambda t, ts=ts, dc=dc, trp=trp: t.transpose(
                    trp[0:128, 0:128], xnf[ts % 2][:, dc * 128:(dc + 1) * 128],
                    ident[:]))
                v_tr_last[ts] = v_t
                if dc % 2 == 0:
                    W("scalar", "ST", v_t)
                    E("scalar", lambda s, ts=ts, dc=dc, trp=trp: s.activation(
                        xn_fm[:, dc * 2048 + ts * 128:dc * 2048 + (ts + 1) * 128],
                        trp[0:128, 0:128], AF.Identity))
                    v_ev1 = I("scalar", "SS", lambda s, k_=k_, dc=dc, trp=trp: s.activation(
                        xfm4[:, k_ * 1024 + dc * 128:k_ * 1024 + (dc + 1) * 128],
                        trp[0:128, 0:128], AF.Identity))
                    tr_evac[tr_idx] = ("SS", v_ev1)
                    v_ev_ss = v_ev1
                else:
                    W("vector", "ST", v_t)
                    E("vector", lambda v, ts=ts, dc=dc, trp=trp: v.tensor_copy(
                        xn_fm[:, dc * 2048 + ts * 128:dc * 2048 + (ts + 1) * 128],
                        trp[0:128, 0:128]))
                    v_ev1 = I("vector", "SV", lambda v, k_=k_, dc=dc, trp=trp: v.tensor_copy(
                        xfm4[:, k_ * 1024 + dc * 128:k_ * 1024 + (dc + 1) * 128],
                        trp[0:128, 0:128]))
                    tr_evac[tr_idx] = ("SV", v_ev1)
                    v_ev_sv = v_ev1
                tr_idx += 1
        v_xnq[q_] = ("SV", C["SV"])
        v_ssq[q_] = ("SS", C["SS"])
        trp = TRS[tr_idx % 4]
        Wtr("tensor", v_xnq[q_] + v_ssq[q_])
        if tr_idx >= 4:
            Wtr("tensor", tr_evac[tr_idx - 4])
        for dc in range(NDC - 1):
            E("tensor", lambda t, dc=dc, trp=trp: t.matmul(
                trp[0:4, 0:512], wr_sb[:, dc * 4:(dc + 1) * 4],
                bass.AP(xfm4, dc * 128, [[4096, 128], [1024, 4], [1, 128]]),
                start=(dc == 0), stop=False))
        v_rt = I("tensor", "ST", lambda t, trp=trp: t.matmul(
            trp[0:4, 0:512], wr_sb[:, 7 * 4:8 * 4],
            bass.AP(xfm4, 7 * 128, [[4096, 128], [1024, 4], [1, 128]]),
            start=False, stop=True))
        W("vector", "ST", v_rt)
        v_lem = I("vector", "SV", lambda v, trp=trp: v.tensor_scalar(
            lgt_em[:], trp[0:4, 0:512], br_em[0:4, 0:1], None, ALU.add))
        tr_evac[tr_idx] = v_lem
        tr_idx += 1
        for k_ in range(4):
            ts = q_ * 4 + k_
            trp = TRS[tr_idx % 4]
            W("tensor", "SV", v_lem)
            if tr_idx >= 4:
                Wtr("tensor", tr_evac[tr_idx - 4])
            v_t = I("tensor", "ST", lambda t, k_=k_, trp=trp: t.transpose(
                trp[0:128, 0:4], lgt_em[0:4, k_ * 128:(k_ + 1) * 128],
                ident[0:4, 0:4]))
            W("vector", "ST", v_t)
            v_lgt[ts] = I("vector", "SV", lambda v, ts=ts, trp=trp: v.tensor_copy(
                lgt[:, ts * 4:(ts + 1) * 4], trp[0:128, 0:4]))
            tr_evac[tr_idx] = v_lgt[ts]
            tr_idx += 1
        if q_ + 1 < 4:
            emit_ln(q_ + 1)
        for fc8 in range(8, 16):
            if tr_evac_chunk[fc8]:
                for tt in range(q_ + 1):
                    if (fc8, tt) not in qk_done and (fc8 - 8) // 4 <= q_:
                        emit_qk_tt(fc8, tt, tr_evac_chunk[fc8],
                                   v_xnq[tt] + v_xnq[(fc8 - 8) // 4] +
                                   v_ssq[tt] + v_ssq[(fc8 - 8) // 4])
                        qk_done.add((fc8, tt))
    v_xn_done = C["SV"]

    lgt3 = lgt[:].rearrange("p (ts k) -> p ts k", k=4)
    cmp3 = cmp[:].rearrange("p (i c) -> p i c", c=16)
    cnt3 = cnt[:].rearrange("p (ts k) -> p ts k", k=4)
    E("vector", lambda v: v.drain())
    for ts in range(NTS):
        E("vector", lambda v, ts=ts: v.tensor_reduce(
            m1a[:, ts:ts + 1], lgt[:, ts * 4:(ts + 1) * 4], AX.X, ALU.max))
    pairs = [(0, 1), (0, 2), (0, 3), (1, 2), (1, 3), (2, 3)]
    for q, (a_, b_) in enumerate(pairs):
        E("vector", lambda v, q=q, a_=a_, b_=b_: v.tensor_tensor(
            cmp3[:, 3 * q, :], lgt3[:, :, a_], lgt3[:, :, b_], ALU.is_gt))
        E("vector", lambda v, q=q, a_=a_, b_=b_: v.tensor_tensor(
            cmp3[:, 3 * q + 1, :], lgt3[:, :, a_], lgt3[:, :, b_], ALU.is_lt))
        E("vector", lambda v, q=q, a_=a_, b_=b_: v.tensor_tensor(
            cmp3[:, 3 * q + 2, :], lgt3[:, :, a_], lgt3[:, :, b_], ALU.is_equal))
    E("vector", lambda v: v.drain())
    terms = {0: [(0, 1), (1, 1), (2, 1)],
             1: [(0, 0), (3, 1), (4, 1), (0, 2)],
             2: [(1, 0), (3, 0), (5, 1), (1, 2), (3, 2)],
             3: [(2, 0), (4, 0), (5, 0), (2, 2), (4, 2), (5, 2)]}
    for e_, tl in terms.items():
        (q0, k0), (q1, k1) = tl[0], tl[1]
        E("vector", lambda v, e_=e_, q0=q0, k0=k0, q1=q1, k1=k1: v.tensor_tensor(
            cnt3[:, :, e_], cmp3[:, 3 * q0 + k0, :], cmp3[:, 3 * q1 + k1, :], ALU.add))
    E("vector", lambda v: v.drain())
    for lev in range(2, max(len(tl) for tl in terms.values())):
        for e_, tl in terms.items():
            if lev < len(tl):
                q_, k_ = tl[lev]
                E("vector", lambda v, e_=e_, q_=q_, k_=k_: v.tensor_tensor(
                    cnt3[:, :, e_], cnt3[:, :, e_], cmp3[:, 3 * q_ + k_, :], ALU.add))
        E("vector", lambda v: v.drain())
    E("vector", lambda v: v.tensor_single_scalar(mask2[:], cnt[:], 1.5, ALU.is_le))
    I("vector", "SV", lambda v: v.tensor_scalar_mul(nm1a[:], m1a[:], -1.0))
    E("vector", lambda v: v.drain())

    for fc8 in range(16):
        for tt in range(4):
            if (fc8, tt) not in qk_done:
                emit_qk_tt(fc8, tt, tr_evac_chunk[fc8],
                           v_xnq[3] + v_ssq[3])
                qk_done.add((fc8, tt))
    emit_v(0, tr_evac_chunk[19] + v_xnq[3] + v_ssq[3])
    emit_v(1, tr_evac_chunk[23] + v_xnq[3] + v_ssq[3])

    P.brk()
    W("sync", "SCW", 16)
    for fc in range(8):
        v_sqw = I("sync", "SQW", lambda e, fc=fc: e.dma_start_transpose(
            bass.AP(w_outT, fc * 128, [[8192, 128], [1024, 8], [1, 128]]),
            woutbf_h.ap()[fc * 128:(fc + 1) * 128, :]), 16)

    w1g = [w1g0, w1g1]
    w2g = [w2g0, w2g1]
    h_sb = [h_sb0, h_sb1]

    def emit_w1_dma(i):
        e_, g_ = divmod(i, NG)
        buf = i % 2
        I("gpsimd", "SW%d" % buf, lambda g, e_=e_, g_=g_, buf=buf: g.dma_start(
            w1g[buf][:].rearrange("p (dc hh) -> p dc hh", hh=1024),
            w1_h.ap()[e_, :, g_ * 1024:(g_ + 1) * 1024]
            .rearrange("(dc p) hh -> p dc hh", p=128)), 16)

    def emit_w2_dma(i):
        e_, g_ = divmod(i, NG)
        buf = i % 2
        return I("gpsimd", "SW%d" % (2 + buf), lambda g, e_=e_, g_=g_, buf=buf: g.dma_start(
            w2g[buf][:].rearrange("p (hc d2) -> p hc d2", d2=1024),
            w2_h.ap()[e_, g_ * 1024:(g_ + 1) * 1024, :]
            .rearrange("(hc p) d2 -> p hc d2", p=128)), 16)

    def emit_slab_dma(i):
        emit_w1_dma(i)
        return emit_w2_dma(i)

    p_sb = [p_sb0, p_sb1]
    rrow = [rrow0, rrow1]
    rln = [rln0, rln1]
    rbr = [rbr0, rbr1]
    rbsb = [rbsb0, rbsb1]
    proj = [proj0, proj1]
    PB = [P45, P67]
    SLOT = [P23[:, 0:512], P23[:, 512:1024], P01[:, 0:512]]
    RBP = P01[:, 512:1024]
    sc_guard = [("SV", tr_evac[tr_idx - 1]), ("SV", tr_evac[tr_idx - 1])]
    pb_guard = [("ST", 0), ("ST", 0)]
    att_guard = [("SV", 0), ("SV", 0), ("SV", 0)]
    rb_guard = ("SS", 0)
    v_outproj = [0] * 4
    v_attnmult_last = [0]
    v_projev = [0, 0]
    v_projdma = [0, 0]

    def flush_stage1(pend):
        head, v_avl = pend[0], pend[4]
        sl = SLOT[slot_of[head]]
        W("vector", "ST", v_avl)
        return I("vector", "SV", lambda v, head=head, sl=sl: v.tensor_copy(
            rrow[head % 2][0:1, :], sl[64:65, :]))

    def flush_stage2(pend):
        nonlocal rb_guard
        head, pair, hr, qt_, v_r = pend[0], pend[1], pend[2], pend[3], pend[5]
        sl = SLOT[slot_of[head]]
        W("scalar", "SV", v_r)
        E("scalar", lambda s, head=head: s.activation(
            rln[head % 2][0:1, :], rrow[head % 2][0:1, :], AF.Ln))
        E("scalar", lambda s: s.drain())
        v_exp = I("scalar", "SS", lambda s, head=head: s.activation(
            rbr[head % 2][0:1, :], rln[head % 2][0:1, :], AF.Exp,
            scale=cneg1[0:1, 0:1]))
        gs, gv = rb_guard
        W("tensor", gs, gv)
        W("tensor", "SS", v_exp)
        v_rb = I("tensor", "ST", lambda t, head=head: t.matmul(
            RBP[0:64, :], ones_bf[0:1, 0:64],
            rbr[head % 2][0:1, :], start=True, stop=True))
        W("vector", "ST", v_rb)
        v_rc = I("vector", "SV", lambda v, head=head: v.tensor_copy(
            rbsb[head % 2][:], RBP[0:64, :]))
        rb_guard = ("SV", v_rc)
        E("vector", lambda v: v.drain())
        if qt_ >= 2:
            W("vector", "ST", v_outproj[qt_ - 2])
        ac = attn_cat if qt_ % 2 == 0 else attn_cat2
        v_am = I("vector", "SV", lambda v, head=head, pair=pair, hr=hr, sl=sl,
                 ac=ac: v.tensor_tensor(
                     ac[hr:hr + 64, pair * 512:(pair + 1) * 512],
                     sl[0:64, :], rbsb[head % 2][:], ALU.mult))
        att_guard[slot_of[head]] = ("SV", v_am)
        v_attnmult_last[0] = v_am

    def emit_pair(qt, head, p, npair, v_use, v_uses):
        bpar = head % 2
        pair = head // 2
        hr = (head % 2) * 64
        pb = PB[bpar]
        psb = p_sb[bpar]
        kb0 = 2 * p
        gs, gv = sc_guard[bpar]
        W("tensor", gs, gv)
        E("tensor", lambda t, kb0=kb0, pb=pb, hr=hr, pair=pair, qt=qt: t.matmul(
            pb[:, 0:512],
            k_fm[hr:hr + 64, pair * 2048 + kb0 * 128:pair * 2048 + (kb0 + 1) * 128],
            q_fm[hr:hr + 64, pair * 2048 + qt * 512:pair * 2048 + (qt + 1) * 512],
            start=True, stop=True))
        v_sc = I("tensor", "ST", lambda t, kb0=kb0, pb=pb, hr=hr, pair=pair, qt=qt:
                 t.matmul(
                     pb[:, 512:1024],
                     k_fm[hr:hr + 64, pair * 2048 + (kb0 + 1) * 128:pair * 2048 + (kb0 + 2) * 128],
                     q_fm[hr:hr + 64, pair * 2048 + qt * 512:pair * 2048 + (qt + 1) * 512],
                     start=True, stop=True))
        spg, spv = pb_guard[bpar]
        W("scalar", "ST", v_sc)
        W("scalar", spg, spv)
        ve = I("scalar", "SS", lambda s, pb=pb, psb=psb: s.activation(
            psb[:], pb[:, 0:1024], AF.Exp, bias=cneg8[:, 0:1]))
        sc_guard[bpar] = ("SS", ve)
        v_use[p] = ve
        v_uses[p] = "SS"
        if p >= npair - 2:
            j = p - (npair - 2)
            W("vector", "SS", ve)
            vm = I("vector", "SV", lambda v, psb=psb, j=j: v.tensor_tensor(
                psb[:], psb[:], masks[:, j * 1024:(j + 1) * 1024], ALU.mult))
            v_use[p] = vm
            v_uses[p] = "SV"

    def emit_av(qt, head, p, npair, v_use, v_uses, v_avp):
        bpar = head % 2
        psb = p_sb[bpar]
        sl = SLOT[slot_of[head]]
        W("tensor", v_uses[p], v_use[p])
        if p == 0:
            gs, gv = att_guard[slot_of[head]]
            W("tensor", gs, gv)
        E("tensor", lambda t, p=p, head=head, psb=psb, sl=sl: t.matmul(
            sl[0:65, :],
            bass.AP(v_ext, 2 * p * VW + head * 65,
                    [[NTS * VW, 128], [1, 65]]),
            psb[:, 0:512], start=(p == 0), stop=False))
        av_stop = (p == npair - 1)
        v_avp[p] = I("tensor", "ST", lambda t, p=p, head=head,
                     psb=psb, sl=sl, av_stop=av_stop: t.matmul(
                         sl[0:65, :],
                         bass.AP(v_ext, (2 * p + 1) * VW + head * 65,
                                 [[NTS * VW, 128], [1, 65]]),
                         psb[:, 512:1024], start=False, stop=av_stop))
        pb_guard[bpar] = ("ST", v_avp[p])

    pending = [None, None]
    op_defer = []
    slot_of = {h: h % 3 for h in range(16)}

    def emit_opuser(qt_src, ts4):
        bpar = ts4 % 2
        pb = PB[bpar]
        W("tensor", "SQW", v_sqw)
        gs, gv = sc_guard[bpar]
        W("tensor", gs, gv)
        if ts4 == 0:
            W("tensor", "SV", v_attnmult_last[0])
        ac = attn_cat if qt_src % 2 == 0 else attn_cat2
        for ec in range(NDC):
            for half in range(2):
                st = (ec == 0)
                sp = (ec == NDC - 1)
                if not (sp and half == 1):
                    E("tensor", lambda t, ts4=ts4, ec=ec, half=half, st=st,
                      sp=sp, pb=pb, ac=ac:
                      t.matmul(pb[:, half * 512:(half + 1) * 512],
                               ac[:, ec * 512 + ts4 * 128:ec * 512 + (ts4 + 1) * 128],
                               w_outT[:, ec * 1024 + half * 512:ec * 1024 + (half + 1) * 512],
                               start=st, stop=sp))
        v_op = I("tensor", "ST", lambda t, ts4=ts4, pb=pb, ac=ac:
                 t.matmul(pb[:, 512:1024],
                          ac[:, 7 * 512 + ts4 * 128:7 * 512 + (ts4 + 1) * 128],
                          w_outT[:, 7 * 1024 + 512:8 * 1024],
                          start=False, stop=True))
        v_outproj[qt_src] = v_op
        W("vector", "ST", v_op)
        if qt_src * 4 + ts4 >= 2:
            W("vector", "SP%d" % (ts4 % 2), v_projdma[ts4 % 2])
        v_pe = I("vector", "SV", lambda v, ts4=ts4, pb=pb: v.tensor_copy(
            proj[ts4 % 2][:], pb[:, 0:1024]))
        v_projev[ts4 % 2] = v_pe
        sc_guard[bpar] = ("SV", v_pe)
        W("sync", "SV", v_pe)
        gts = qt_src * 4 + ts4
        v_projdma[ts4 % 2] = I("sync", "SP%d" % (ts4 % 2), lambda e, gts=gts, ts4=ts4:
                               e.dma_start(
                                   attn_sc.ap()[gts * 128:(gts + 1) * 128, :],
                                   proj[ts4 % 2][:]), 16)

    for qt in range(4):
        kbmax = 4 * (qt + 1)
        npair = kbmax // 2
        hporder = [6, 7, 0, 1, 2, 3, 4, 5] if qt == 3 else list(range(NH // 2))
        for i_hp, hp in enumerate(hporder):
            h0 = 2 * hp
            h1 = 2 * hp + 1
            slot_of[h0] = (2 * i_hp) % 3
            slot_of[h1] = (2 * i_hp + 1) % 3
            u0, s0, a0 = [0] * npair, ["SS"] * npair, [0] * npair
            u1, s1_, a1 = [0] * npair, ["SS"] * npair, [0] * npair
            emit_pair(qt, h0, 0, npair, u0, s0)
            emit_pair(qt, h1, 0, npair, u1, s1_)
            for p in range(npair):
                emit_av(qt, h0, p, npair, u0, s0, a0)
                if p == 0 and pending[0] is not None:
                    flush_stage2(pending[0])
                if p + 1 < npair:
                    emit_pair(qt, h0, p + 1, npair, u0, s0)
                emit_av(qt, h1, p, npair, u1, s1_, a1)
                if p + 1 < npair:
                    emit_pair(qt, h1, p + 1, npair, u1, s1_)
                if op_defer and i_hp == 0:
                    op_defer.pop(0)()
            if pending[0] is not None:
                flush_stage2(pending[1])
            while op_defer and i_hp == 0:
                op_defer.pop(0)()
            pend0 = [h0, h0 // 2, (h0 % 2) * 64, qt, a0[npair - 1], 0]
            pend1 = [h1, h1 // 2, (h1 % 2) * 64, qt, a1[npair - 1], 0]
            pend0[5] = flush_stage1(pend0)
            pend1[5] = flush_stage1(pend1)
            pending = [pend0, pend1]
            if qt == 3 and i_hp == 1:
                W("gpsimd", "ST", C["ST"])
                I("gpsimd", "SW0", lambda g: g.dma_start(
                    w1g0[:, 0:4096].rearrange("p (dc hh) -> p dc hh", hh=1024),
                    w1_h.ap()[0, 0:512, 0:1024]
                    .rearrange("(dc p) hh -> p dc hh", p=128)), 16)
            if qt == 3 and i_hp == 3:
                W("gpsimd", "ST", C["ST"])
                I("gpsimd", "SW0", lambda g: g.dma_start(
                    w1g0[:, 4096:8192].rearrange("p (dc hh) -> p dc hh", hh=1024),
                    w1_h.ap()[0, 512:1024, 0:1024]
                    .rearrange("(dc p) hh -> p dc hh", p=128)), 16)
        if qt == 3:
            W("gpsimd", "ST", pending[1][4])
            emit_w2_dma(0)
            emit_w1_dma(1)
            emit_w2_dma(1)
        flush_stage2(pending[0])
        flush_stage2(pending[1])
        pending = [None, None]
        op_defer = [(lambda qt=qt, ts4=ts4: emit_opuser(qt, ts4)) for ts4 in range(4)]
        if qt == 3:
            for f in op_defer:
                f()
            op_defer = []

    v_ex_last = 0
    for ts in range(NTS):
        v_ex_last = I("scalar", "SS", lambda s, ts=ts: s.activation(
            exa[:, ts * 4:(ts + 1) * 4], lgt[:, ts * 4:(ts + 1) * 4], AF.Exp,
            bias=nm1a[:, ts:ts + 1]))
    W("vector", "SS", v_ex_last)
    E("vector", lambda v: v.tensor_tensor(g2a[:], exa[:], mask2[:], ALU.mult))
    E("vector", lambda v: v.drain())
    for ts in range(NTS):
        E("vector", lambda v, ts=ts: v.tensor_reduce(
            Sa[:, ts:ts + 1], g2a[:, ts * 4:(ts + 1) * 4], AX.X, ALU.add))
    E("vector", lambda v: v.drain())
    E("vector", lambda v: v.reciprocal(rSa[:], Sa[:]))
    E("vector", lambda v: v.drain())
    v_ag = [0] * NTS
    for ts in range(NTS):
        v_ag[ts] = I("vector", "SV", lambda v, ts=ts: v.tensor_scalar_mul(
            agate[:, ts * 4:(ts + 1) * 4], g2a[:, ts * 4:(ts + 1) * 4],
            rSa[:, ts:ts + 1]))
    W("tensor", *sc_guard[0])
    W("tensor", *sc_guard[1])
    for ts in range(NTS):
        trp = TRS[tr_idx % 4]
        if tr_idx >= 4:
            Wtr("tensor", tr_evac[tr_idx - 4])
        W("tensor", "SV", v_ag[ts])
        v_t = I("tensor", "ST", lambda t, ts=ts, trp=trp: t.transpose(
            trp[0:4, 0:128], agate[:, ts * 4:(ts + 1) * 4], ident[:]))
        W("vector", "ST", v_t)
        v_ev = I("vector", "SV", lambda v, ts=ts, trp=trp: v.tensor_copy(
            aem[:, ts * 128:(ts + 1) * 128], trp[0:4, 0:128]))
        tr_evac[tr_idx] = v_ev
        tr_idx += 1

    P.brk()
    W("vector", "SP0", C["SP0"])
    W("vector", "SP1", C["SP1"])
    v_hfake = I("scalar", "SS", lambda s: s.activation(hfake[:], b1_sb[:], AF.Gelu))

    steps = [(i, tt) for i in range(16) for tt in range(NTT)]
    gh_idx = [0]
    gh_gelu = {}
    gy_idx = [0]
    gy_evic = {}
    hsb_lastread = {0: 0, 1: 0}
    v_lastuse = [0] * 16
    v_gevac = [0] * NE

    def emit_h(step):
        i, tt = steps[step]
        e_, g_ = divmod(i, NG)
        buf = i % 2
        if tt == 0:
            W("tensor", "SW%d" % (i % 2),
              16 * (i // 2 + 1) + (16 if i % 2 == 0 else 0))
        for hc in range(NHC):
            par = gh_idx[0] % 2
            W("tensor", "SS", gh_gelu.get(gh_idx[0] - 2, 0))
            v_g = I("tensor", "ST", lambda t, buf=buf, tt=tt, hc=hc, par=par: [
                t.matmul(PH67[par][:, 0:512],
                         w1g[buf][:, dc * 1024 + hc * 128:dc * 1024 + (hc + 1) * 128],
                         xn_fm[:, dc * 2048 + tt * 512:dc * 2048 + (tt + 1) * 512],
                         start=(dc == 0), stop=(dc == NDC - 1))
                for dc in range(NDC)][-1])
            W("scalar", "ST", v_g)
            W("scalar", "ST", hsb_lastread[tt % 2])
            vgl = I("scalar", "SS", lambda s, tt=tt, hc=hc, par=par, e_=e_, g_=g_:
                    s.activation(h_sb[tt % 2][:, hc * 512:(hc + 1) * 512],
                                 PH67[par][:, 0:512], AF.Gelu,
                                 bias=b1_sb[:, e_ * 32 + g_ * 8 + hc:e_ * 32 + g_ * 8 + hc + 1]))
            gh_gelu[gh_idx[0]] = vgl
            gh_idx[0] += 1

    def emit_y(step):
        i, tt = steps[step]
        e_, g_ = divmod(i, NG)
        buf = i % 2
        if tt == 0:
            W("tensor", "SW%d" % (2 + i % 2), 16 * (i // 2 + 1))
        W("tensor", "SS", gh_gelu[step * NHC + NHC - 1])
        for ts4 in range(4):
            tsg = tt * 4 + ts4
            pt = P01 if gy_idx[0] % 2 == 0 else P23
            W("tensor", "SV", gy_evic.get(gy_idx[0] - 2, 0))
            v_g = 0
            for hc in range(NHC):
                for half in range(2):
                    st = (hc == 0)
                    sp = (hc == NHC - 1)
                    fn = (lambda t, buf=buf, tt=tt, ts4=ts4, hc=hc, half=half,
                          pt=pt, st=st, sp=sp: t.matmul(
                              pt[:, half * 512:(half + 1) * 512],
                              h_sb[tt % 2][:, hc * 512 + ts4 * 128:hc * 512 + (ts4 + 1) * 128],
                              w2g[buf][:, hc * 1024 + half * 512:hc * 1024 + (half + 1) * 512],
                              start=st, stop=sp))
                    if sp and half == 1:
                        v_g = I("tensor", "ST", fn)
                    else:
                        E("tensor", fn)
            v_lastuse[i] = v_g
            hsb_lastread[tt % 2] = v_g
            W("vector", "ST", v_g)
            if i == 0:
                vev = I("vector", "SV", lambda v, tsg=tsg, e_=e_, pt=pt:
                        v.tensor_scalar_mul(
                            accum[:, tsg * 1024:(tsg + 1) * 1024], pt[:, 0:1024],
                            agate[:, tsg * 4 + e_:tsg * 4 + e_ + 1]))
            else:
                vev = I("vector", "SV", lambda v, tsg=tsg, e_=e_, pt=pt:
                        v.scalar_tensor_tensor(
                            accum[:, tsg * 1024:(tsg + 1) * 1024], pt[:, 0:1024],
                            agate[:, tsg * 4 + e_:tsg * 4 + e_ + 1],
                            accum[:, tsg * 1024:(tsg + 1) * 1024],
                            ALU.mult, ALU.add))
            gy_evic[gy_idx[0]] = vev
            gy_idx[0] += 1

    def emit_g(i):
        e_, g_ = divmod(i, NG)
        buf = i % 2
        W("tensor", "SW%d" % (2 + i % 2), 16 * (i // 2 + 1))
        if g_ == 0:
            if e_ > 0:
                W("tensor", "SV", v_gevac[e_ - 1])
            else:
                W("tensor", "SS", v_hfake)
        v_g = 0
        for hc in range(NHC):
            for half, pt in ((0, P4), (1, P5)):
                st = (g_ == 0 and hc == 0)
                sp = (g_ == NG - 1 and hc == NHC - 1)
                fn = (lambda t, buf=buf, e_=e_, g_=g_, hc=hc, pt=pt, half=half,
                      st=st, sp=sp: t.matmul(
                          pt[0:1, 0:512],
                          hfake[:, e_ * 32 + g_ * 8 + hc:e_ * 32 + g_ * 8 + hc + 1],
                          w2g[buf][:, hc * 1024 + half * 512:hc * 1024 + (half + 1) * 512],
                          start=st, stop=sp))
                if hc == NHC - 1 and half == 1:
                    v_g = I("tensor", "ST", fn)
                else:
                    E("tensor", fn)
        v_lastuse[i] = v_g
        if g_ == NG - 1:
            W("vector", "ST", v_g)
            E("vector", lambda v, e_=e_: v.tensor_copy(
                g_sb[0:1, e_ * 1024:e_ * 1024 + 512], P4[0:1, 0:512]))
            v_gevac[e_] = I("vector", "SV", lambda v, e_=e_: v.tensor_copy(
                g_sb[0:1, e_ * 1024 + 512:(e_ + 1) * 1024], P5[0:1, 0:512]))

    emit_h(0)
    for step in range(len(steps)):
        if step + 1 < len(steps):
            emit_h(step + 1)
        emit_y(step)
        i, tt = steps[step]
        if tt == NTT - 1:
            emit_g(i)
            if i + 2 < 16:
                W("gpsimd", "ST", v_lastuse[i])
                emit_slab_dma(i + 2)

    v_lasth = gh_gelu[gh_idx[0] - 1]
    W("sync", "SS", v_lasth)
    v_b2 = I("sync", "SX", lambda e: e.dma_start(b2_sb[:], b2_h.ap()), 16)
    W("tensor", "SV", v_gevac[NE - 1])
    W("tensor", "SV", gy_evic[gy_idx[0] - 2])
    W("tensor", "SV", gy_evic[gy_idx[0] - 1])
    v_g4 = 0
    for e_ in range(NE):
        for half in range(2):
            st = (e_ == 0)
            sp = (e_ == NE - 1)
            fn = (lambda t, e_=e_, half=half, st=st, sp=sp: t.matmul(
                P01[0:4, half * 512:(half + 1) * 512],
                oh[0:1, e_ * 4:(e_ + 1) * 4],
                g_sb[0:1, e_ * 1024 + half * 512:e_ * 1024 + (half + 1) * 512],
                start=st, stop=sp))
            if sp and half == 1:
                v_g4 = I("tensor", "ST", fn)
            else:
                E("tensor", fn)
    W("vector", "ST", v_g4)
    v_g4c = I("vector", "SV", lambda v: v.tensor_copy(g4_sb[:], P01[0:4, 0:1024]))
    E("vector", lambda v: v.drain())
    W("vector", "SX", v_b2)
    v_t4 = I("vector", "SV", lambda v: v.tensor_tensor(
        tmp4[:], g4_sb[:], b2_sb[:], ALU.add))
    W("tensor", "SV", v_t4)
    v_cs = 0
    for half in range(2):
        v_cs = I("tensor", "ST", lambda t, half=half: t.matmul(
            P23[0:1, half * 512:(half + 1) * 512], ones_p[0:4, 0:1],
            tmp4[0:4, half * 512:(half + 1) * 512], start=True, stop=True))
    W("vector", "ST", v_cs)
    v_csc = I("vector", "SV", lambda v: v.tensor_copy(csum_sb[:], P23[0:1, 0:1024]))
    W("tensor", "SV", v_csc)
    W("tensor", "SV", v_g4c)
    v_cb = 0
    for half in range(2):
        v_cb = I("tensor", "ST", lambda t, half=half: t.matmul(
            P01[0:4, half * 512:(half + 1) * 512], ones_f[0:1, 0:4],
            csum_sb[0:1, half * 512:(half + 1) * 512], start=True, stop=True))
    W("vector", "ST", v_cb)
    v_b2pv = I("vector", "SV", lambda v: v.tensor_tensor(
        b2p_sb[:], P01[0:4, 0:1024], g4_sb[:], ALU.subtract))
    E("vector", lambda v: v.drain())
    W("tensor", "SV", v_b2pv)
    v_bev = [0, 0, 0, 0]
    for ts in range(NTS):
        pt = [P01, P23, P45, P67][ts % 4]
        if ts >= 4:
            W("tensor", "SV", v_bev[ts % 4])
        v_bias = 0
        for half in range(2):
            v_bias = I("tensor", "ST", lambda t, ts=ts, half=half, pt=pt: t.matmul(
                pt[:, half * 512:(half + 1) * 512],
                aem[:, ts * 128:(ts + 1) * 128],
                b2p_sb[0:4, half * 512:(half + 1) * 512], start=True, stop=True))
        W("vector", "ST", v_bias)
        v_bev[ts % 4] = I("vector", "SV", lambda v, ts=ts, pt=pt: v.scalar_tensor_tensor(
            accum[:, ts * 1024:(ts + 1) * 1024], pt[:, 0:1024], 1.0,
            accum[:, ts * 1024:(ts + 1) * 1024], ALU.mult, ALU.add))

    P.brk()
    W("sync", "SP0", C["SP0"])
    W("sync", "SP1", C["SP1"])
    v_xg = [0] * 4
    v_ag = [0] * 4
    v_estt = [0] * NTS

    def emit_egrp(grp, guard):
        if guard:
            W("sync", "SV", guard)
        v_xg[grp] = I("sync", "SXR", lambda e, grp=grp: e.dma_start(
            xrg[grp % 2][:].rearrange("p (q j) -> p q j", j=1024),
            x_h.ap()[grp * 512:(grp + 1) * 512, :]
            .rearrange("(q p) j -> p q j", p=128)), 16)
        v_ag[grp] = I("sync", "SAR", lambda e, grp=grp: e.dma_start(
            arg[grp % 2][:].rearrange("p (q j) -> p q j", j=1024),
            attn_sc.ap()[grp * 512:(grp + 1) * 512, :]
            .rearrange("(q p) j -> p q j", p=128)), 16)

    emit_egrp(0, 0)
    emit_egrp(1, 0)
    for ts in range(NTS):
        grp = ts // 4
        q4 = ts % 4
        W("vector", "SXR", v_xg[grp])
        W("vector", "SAR", v_ag[grp])
        if ts >= 4:
            W("vector", "SO", (ts - 4 + 1) * 16)
        E("vector", lambda v, ts=ts, grp=grp, q4=q4: v.tensor_tensor(
            outr[ts % 4][:], xrg[grp % 2][:, q4 * 1024:(q4 + 1) * 1024],
            arg[grp % 2][:, q4 * 1024:(q4 + 1) * 1024], ALU.add))
        E("vector", lambda v: v.drain())
        v_estt[ts] = I("vector", "SV", lambda v, ts=ts: v.tensor_tensor(
            outr[ts % 4][:], outr[ts % 4][:],
            accum[:, ts * 1024:(ts + 1) * 1024], ALU.add))
        W("sync", "SV", v_estt[ts])
        I("sync", "SO", lambda e, ts=ts: e.dma_start(
            y_h.ap()[ts * 128:(ts + 1) * 128, :], outr[ts % 4][:]), 16)
        if ts == 3:
            emit_egrp(2, v_estt[3])
        elif ts == 7:
            emit_egrp(3, v_estt[7])
    W("sync", "SO", 16 * NTS)

    P.run(nc)
    return nc, ctx


_CACHE = {}


def kernel(L0_attn_x, L0_attn_w_qkv, L0_attn_w_out, L0_ffn_w1, L0_ffn_b1,
           L0_ffn_w2, L0_ffn_b2, L0_ffn_wr, L0_ffn_br):
    if "nc" not in _CACHE:
        _CACHE["nc"] = build_nc()
    nc, _ctx = _CACHE["nc"]

    x = np.ascontiguousarray(np.asarray(L0_attn_x, dtype=np.float32))
    wqkv = np.ascontiguousarray(np.asarray(L0_attn_w_qkv, dtype=np.float32))
    wout = np.ascontiguousarray(np.asarray(L0_attn_w_out, dtype=np.float32))
    w1 = np.ascontiguousarray(np.asarray(L0_ffn_w1, dtype=np.float32))
    b1 = np.ascontiguousarray(np.asarray(L0_ffn_b1, dtype=np.float32))
    w2 = np.ascontiguousarray(np.asarray(L0_ffn_w2, dtype=np.float32))
    b2 = np.ascontiguousarray(np.asarray(L0_ffn_b2, dtype=np.float32))
    wr = np.ascontiguousarray(np.asarray(L0_ffn_wr, dtype=np.float32))
    br = np.ascontiguousarray(np.asarray(L0_ffn_br, dtype=np.float32))

    in_maps = [{"x": x[c], "w_qkv": wqkv, "w_out": wout, "w1": w1, "b1": b1,
                "w2": w2, "b2": b2, "wr": wr, "br": br} for c in range(8)]
    res = run_bass_kernel_spmd(nc, in_maps, core_ids=list(range(8)))
    return np.stack([np.asarray(r["y"]) for r in res.results], axis=0)
